# version 23
# baseline (speedup 1.0000x reference)
import numpy as np
import ml_dtypes
import concourse.bass as bass
import concourse.mybir as mybir
from concourse.bass_utils import run_bass_kernel_spmd
from concourse.bass_core_rust import IndirectOffsetOnAxis
from contextlib import ExitStack

F32 = mybir.dt.float32
BF16 = mybir.dt.bfloat16
I32 = mybir.dt.int32
OP = mybir.AluOpType
AF = mybir.ActivationFunctionType
AX = mybir.AxisListType.X

N_CORES = 8
B, S, D = 16, 2048, 1024
TOK = 4096
NTILE = 32
KCH = 8
NEXP, DFF, NDQ, DQ = 8, 4096, 4, 1024
CLOC = 768
NSLOT = NEXP * CLOC
NSLT = NSLOT // 128
CAP = 5120.0
EPS = 1e-5
SIM_SINGLE = False


class Sem:
    def __init__(self, stack, nc, name):
        self.h = stack.enter_context(nc.semaphore(name))
        self.val = 0

    def reserve(self, n=1):
        self.val += n
        return self.val


def build_program():
    nc = bass.Bass("TRN2", target_bir_lowering=False)

    x_h = nc.dram_tensor("x", [TOK, D], F32, kind="ExternalInput").ap()
    wqkvT_h = nc.dram_tensor("wqkvT", [128, 3 * 512], F32, kind="ExternalInput").ap()
    wgT_h = nc.dram_tensor("wgT", [128, 64], F32, kind="ExternalInput").ap()
    b1T_h = nc.dram_tensor("b1T", [128, 256], F32, kind="ExternalInput").ap()
    b2_h = nc.dram_tensor("b2", [NEXP, D], F32, kind="ExternalInput").ap()
    w1_h = nc.dram_tensor("w1", [NEXP, NDQ, 128, KCH, DQ], F32, kind="ExternalInput").ap()
    w2_h = nc.dram_tensor("w2", [NEXP, NDQ, 128, 8, D], F32, kind="ExternalInput").ap()
    id128_h = nc.dram_tensor("id128", [128, 128], F32, kind="ExternalInput").ap()
    lstrict_h = nc.dram_tensor("lstrict", [128, 128], F32, kind="ExternalInput").ap()
    allone_h = nc.dram_tensor("allone", [128, 128], F32, kind="ExternalInput").ap()
    rep64_h = nc.dram_tensor("rep64", [64, D], F32, kind="ExternalInput").ap()
    premask_h = nc.dram_tensor("premask", [8, 1], F32, kind="ExternalInput").ap()
    myhot_h = nc.dram_tensor("myhot", [8, 1], F32, kind="ExternalInput").ap()
    y_h = nc.dram_tensor("y", [TOK, D], F32, kind="ExternalOutput").ap()
    cc_in = nc.dram_tensor("cc_in", [8, 8], F32)
    cc_out = nc.dram_tensor("cc_out", [8, 8], F32)
    xg_h = nc.dram_tensor("xg", [NSLOT + 1, D], BF16).ap()
    og_h = nc.dram_tensor("og", [NSLOT + 1, D], F32).ap()

    with ExitStack() as stack:
        sb = lambda n, s, d: stack.enter_context(nc.sbuf_tensor(n, s, d))
        sem = lambda n: Sem(stack, nc, n)

        bigX = sb("bigX", [128, KCH * NSLOT], BF16)
        w1b = [sb(f"w1b{i}", [128, KCH * DQ], BF16) for i in range(2)]
        w2b = sb("w2b", [128, 8 * D], BF16)
        hbuf = sb("hbuf", [128, 8 * CLOC], BF16)
        etile = [sb(f"etile{i}", [128, 2048], BF16) for i in range(2)]
        stg = [sb(f"stg{i}", [128, D], F32) for i in range(2)]
        gbufs = sb("gbuf", [128, D], F32)
        gbuf2 = nc.alloc_sbuf_tensor_at("gbuf2", [128, D], F32,
                                        offset=nc.lookup_mloc(etile[1]).addr)
        gbuf = [gbufs, gbuf2]
        qk1 = [nc.alloc_sbuf_tensor_at(f"qk1_{i}", [64, 2048], BF16,
                                       offset=nc.lookup_mloc(stg[i]).addr)
               for i in range(2)]
        v1buf = nc.alloc_sbuf_tensor_at("v1buf", [128, D], F32,
                                        offset=nc.lookup_mloc(w1b[1]).addr)
        gbuf3 = nc.alloc_sbuf_tensor_at("gbuf3", [128, D], F32,
                                        offset=nc.lookup_mloc(w2b).addr)
        gbuf4 = nc.alloc_sbuf_tensor_at("gbuf4", [128, D], F32,
                                        offset=nc.lookup_mloc(w2b).addr + 4096)
        ob3 = nc.alloc_sbuf_tensor_at("ob3", [128, D], F32,
                                      offset=nc.lookup_mloc(w1b[0]).addr)
        ob4 = nc.alloc_sbuf_tensor_at("ob4", [128, D], F32,
                                      offset=nc.lookup_mloc(w1b[0]).addr + 4096)
        attnF = sb("attnF", [64, TOK], BF16)
        id128 = sb("id128s", [128, 128], BF16)
        lstrict = sb("lstricts", [128, 128], BF16)
        allone = sb("allones", [128, 128], BF16)
        rep64 = sb("rep64s", [64, D], BF16)
        wqkvT = sb("wqkvTs", [128, 3 * 512], BF16)
        wgT = sb("wgTs", [128, 64], BF16)
        b1T = sb("b1Ts", [128, 16], F32)
        b2b = sb("b2b", [1, D], BF16)
        ones1 = sb("ones1", [1, 128], F32)
        premask = sb("premasks", [8, 1], F32)
        myhot = sb("myhots", [8, 1], F32)
        iota8f = sb("iota8", [128, 12], F32)
        iota8 = iota8f[:, 0:8]
        maskb = sb("maskb", [128, NTILE * 8], BF16)
        g1 = sb("g1", [128, NTILE], F32)
        rankb = sb("rankb", [128, NTILE], F32)
        coefb = sb("coefb", [128, NTILE], F32)
        sloti = sb("sloti", [128, NTILE], I32)
        cntrun = sb("cntrun", [1, 8], F32)
        cntmat = sb("cntmat", [8, 8], F32)
        ccin_s = sb("ccin_s", [8, 8], F32)
        scr = sb("scr", [128, 2 * 1024], F32)
        scrb = sb("scrb", [128, 2048], BF16)

        _hb = nc.lookup_mloc(hbuf).addr
        xbufs = [scr[:, 0:1024], scr[:, 1024:2048],
                 nc.alloc_sbuf_tensor_at("xal0", [128, D], F32, offset=_hb)[:, :],
                 nc.alloc_sbuf_tensor_at("xal1", [128, D], F32, offset=_hb + 4096)[:, :]]
        nbufs = [scrb[:, 0:1024], scrb[:, 1024:2048],
                 nc.alloc_sbuf_tensor_at("nal0", [128, D], BF16, offset=_hb + 8192)[:, :],
                 nc.alloc_sbuf_tensor_at("nal1", [128, D], BF16, offset=_hb + 10240)[:, :]]
        den = sb("den", [128, 16], F32)
        rden = sb("rden", [128, 16], F32)
        ssum = sb("ssum", [128, NTILE], F32)
        ssq = sb("ssq", [128, NTILE], F32)
        negm = sb("negm", [128, NTILE], F32)
        varv = sb("varv", [128, NTILE], F32)
        sdev = varv
        rstd = sb("rstd", [128, NTILE], F32)
        lg = sb("lg", [128, 16], F32)
        gexp = sb("gexp", [128, 16], F32)
        sexp = sb("sexp", [128, NTILE], F32)
        ngmx = sb("ngmx", [128, NTILE], F32)
        gmx = ssum
        tmp8 = sb("tmp8", [128, 8], F32)
        tmp8b = sb("tmp8b", [128, 8], F32)
        tmp1 = den[:, 0:4]
        epsb = iota8f[:, 8:9]
        posb = sexp
        keepb = gmx

        s_bar = sem("bar")
        s_eps = sem("eps")
        s_c = sem("cload")
        s_ci = sem("ciota")
        s_z = sem("zfill")
        s_zg = sem("zfillg")
        s_xin = [sem("xin0"), sem("xin1"), sem("xin2"), sem("xin3")]
        s_xfree = sem("xfree")
        s_vst = sem("vst")
        s_sqr = sem("sqr")
        s_xn = sem("xn")
        s_tr = sem("tr")
        s_trv = {"s": sem("trvs"), "v": sem("trvv"), "g": sem("trvg")}
        s_gst = sem("gst")
        s_mmg = sem("mmg")
        s_vg = sem("vg")
        s_sg = sem("sg")
        s_mmp = sem("mmp")
        s_idx = sem("idx")
        s_sct = [sem("sct0"), sem("sct1"), sem("sct2"), sem("sct3")]
        s_cc = sem("cc")
        s_ccd = sem("ccd")
        s_mma = sem("mma")
        s_va = sem("va")
        s_sa = sem("sa")
        s_gl = [sem("gl0"), sem("gl1")]
        s_tr2 = sem("tr2")
        s_trv2 = sem("trv2")
        s_w = sem("w")
        s_mmh = sem("mmh")
        s_relu = sem("relu")
        s_mmo = sem("mmo")
        s_oev = sem("oev")
        s_oac = [sem("oac0"), sem("oac1")]
        s_oacy = [sem("oacy0"), sem("oacy1")]
        s_og = [sem("og0"), sem("og1"), sem("og2"), sem("og3")]
        s_rep = sem("rep")
        s_out = sem("out")
        s_st = [sem("st0"), sem("st1"), sem("st2"), sem("st3")]

        TX, VX, SX, GX, YX = [], [], [], [], []

        _fsems = {k: stack.enter_context(nc.semaphore(f"flow_{k}"))
                  for k in ("t", "v", "s", "g", "y")}
        _fcnt = {k: 0 for k in _fsems}

        def flow(eng, key):
            _fcnt[key] += 1
            eng.drain().then_inc(_fsems[key], 1)
            eng.wait_ge(_fsems[key], _fcnt[key])

        def edge(eng, key, inst):
            _fcnt[key] += 1
            inst.then_inc(_fsems[key], 1)
            eng.wait_ge(_fsems[key], _fcnt[key])

        def cover(eng, key):
            eng.wait_ge(_fsems[key], _fcnt[key])

        def v_eps(v):
            v.memset(epsb, EPS)
            v.memset(etile[0][:, :], 0.0)
            v.memset(cntrun[:, :], 0.0)
            v.drain().then_inc(s_eps.h, 1)
        VX.append(v_eps)
        s_eps.reserve(1)

        STATS_ON_POOL = True
        AXC = mybir.AxisListType.C
        nbar = [0]

        def barrier():
            nbar[0] += 1
            n = nbar[0]
            for L in (TX, VX, SX, GX, YX):
                def f(e, n=n):
                    e.drain().then_inc(s_bar.h, 1)
                    e.wait_ge(s_bar.h, 5 * n)
                L.append(f)

        def g_consts(g):
            g.dma_start(out=wqkvT[:, :], in_=wqkvT_h).then_inc(s_c.h, 16)
            g.dma_start(out=wgT[:, :], in_=wgT_h).then_inc(s_c.h, 16)
            g.dma_start(out=b1T[:, 0:8], in_=b1T_h[:, 0:8]).then_inc(s_c.h, 16)
            g.dma_start(out=id128[:, :], in_=id128_h).then_inc(s_c.h, 16)
            g.dma_start(out=lstrict[:, :], in_=lstrict_h).then_inc(s_c.h, 16)
            g.dma_start(out=allone[:, :], in_=allone_h).then_inc(s_c.h, 16)
            g.dma_start(out=rep64[:, :], in_=rep64_h).then_inc(s_c.h, 16)
            g.dma_start(out=ones1[:, :], in_=allone_h[0:1, :]).then_inc(s_c.h, 16)
            g.dma_start(out=premask[:, :], in_=premask_h).then_inc(s_c.h, 16)
            g.dma_start(out=myhot[:, :], in_=myhot_h).then_inc(s_c.h, 16)
            g.iota(iota8, pattern=[[1, 8]], base=0, channel_multiplier=0,
                   allow_small_or_imprecise_dtypes=True).then_inc(s_ci.h, 1)
            g.dma_start(out=b2b[:, :], in_=b2_h[0:1, :]).then_inc(s_w.h, 16)
            g.dma_start(out=w1b[0][:, :], in_=w1_h[0, 0]).then_inc(s_w.h, 16)
            g.dma_start(out=w2b[:, :], in_=w2_h[0, 0]).then_inc(s_w.h, 16)
            g.wait_ge(s_eps.h, 1)
            g.dma_start(out=og_h[NSLOT:NSLOT + 1, :], in_=etile[0][0:1, 0:D]).then_inc(s_zg.h, 16)
        GX.append(g_consts)
        s_zg.reserve(16)

        def y_zfill(y):
            y.wait_ge(s_eps.h, 1)
            for blk in range(25):
                r0 = blk * 256
                r1 = min(r0 + 256, NSLOT + 1)
                n_el = (r1 - r0) * D
                zsrc = etile[0][:, :] if n_el == 128 * 2048 else etile[0][0:1, 0:n_el]
                y.dma_start(out=xg_h[r0:r1, :], in_=zsrc).then_inc(s_z.h, 16)
        YX.append(y_zfill)
        s_c.reserve(10 * 16)
        s_ci.reserve(1)
        s_w.reserve(3 * 16)
        zfv = s_z.reserve(25 * 16)

        ps1 = ExitStack()
        PTR_SLOTS = 3
        ptr_sl = [ps1.enter_context(nc.psum_tensor(f"ptr{i}", [128, 128], BF16))[:, :]
                  for i in range(PTR_SLOTS)]
        pcnt0 = ps1.enter_context(nc.psum_tensor("pcnt0", [1, 8], F32))
        plog = [ps1.enter_context(nc.psum_tensor(f"plog{i}", [128, 8], F32)) for i in range(2)]
        ppref = [ps1.enter_context(nc.psum_tensor(f"ppref{i}", [128, 8], F32)) for i in range(2)]
        vg_done_log = [None, None]
        vg_done_pref = [0, 0]

        def y_xload(t, slot, v_need):
            def f(y, t=t, slot=slot, v_need=v_need):
                if v_need:
                    y.wait_ge(s_xn.h, v_need)
                y.dma_start(out=xbufs[slot],
                            in_=x_h[t * 128:(t + 1) * 128, :]).then_inc(s_xin[slot].h, 16)
            YX.append(f)
            return s_xin[slot].reserve(16)

        NS = 4
        xin_val = [0] * NTILE
        for _t in range(NS):
            xin_val[_t] = y_xload(_t, _t, 0)
        sct_of_tile = [0] * (NTILE + NS)
        TS = [dict() for _ in range(NTILE)]

        def st_AB(t):
            slot = t % NS
            xiv = xin_val[t]

            def c_stat(s, t=t, slot=slot, xiv=xiv):
                xt = xbufs[slot]
                s.wait_ge(s_xin[slot].h, xiv)
                s.activation(stg[0][:, :], xt, AF.Copy,
                             accum_out=ssum[:, t:t + 1]).then_inc(s_vst.h, 1)
                s.activation(stg[1][:, :], xt, AF.Square,
                             accum_out=ssq[:, t:t + 1]).then_inc(s_vst.h, 1)
            SX.append(c_stat)
            vstv = s_vst.reserve(2)

            if STATS_ON_POOL:
                def g_stats(g, t=t, vstv=vstv):
                    g.wait_ge(s_vst.h, vstv)
                    edge(g, "g", g.tensor_scalar_mul(negm[:, t:t + 1], ssum[:, t:t + 1],
                                                     -1.0 / D))
                    edge(g, "g", g.tensor_mul(rstd[:, t:t + 1], negm[:, t:t + 1],
                                              negm[:, t:t + 1]))
                    edge(g, "g", g.tensor_scalar_mul(varv[:, t:t + 1], ssq[:, t:t + 1],
                                                     1.0 / D))
                    g.tensor_sub(varv[:, t:t + 1], varv[:, t:t + 1],
                                 rstd[:, t:t + 1]).then_inc(s_gst.h, 1)
                GX.append(g_stats)
            else:
                def v_stats(v, t=t, vstv=vstv):
                    v.wait_ge(s_vst.h, vstv)
                    edge(v, "v", v.tensor_scalar(negm[:, t:t + 1], ssum[:, t:t + 1],
                                                 -1.0 / D, None, OP.mult))
                    edge(v, "v", v.tensor_mul(rstd[:, t:t + 1], negm[:, t:t + 1],
                                              negm[:, t:t + 1]))
                    edge(v, "v", v.tensor_scalar(varv[:, t:t + 1], ssq[:, t:t + 1],
                                                 1.0 / D, None, OP.mult))
                    v.tensor_sub(varv[:, t:t + 1], varv[:, t:t + 1],
                                 rstd[:, t:t + 1]).then_inc(s_gst.h, 1)
                VX.append(v_stats)
            TS[t]["vvv2"] = s_gst.reserve()

        def st_CD(t):
            slot = t % NS
            xiv = xin_val[t]
            sct_prev = sct_of_tile[t]
            vvv2 = TS[t]["vvv2"]

            def c_rstd(s, t=t, vvv2=vvv2):
                s.wait_ge(s_eps.h, 1)
                s.wait_ge(s_gst.h, vvv2)
                edge(s, "s", s.activation(sdev[:, t:t + 1], varv[:, t:t + 1], AF.Ln,
                                          bias=epsb, scale=1.0))
                s.activation(rstd[:, t:t + 1], sdev[:, t:t + 1], AF.Exp,
                             bias=0.0, scale=-0.5).then_inc(s_sqr.h, 1)
            SX.append(c_rstd)
            sqv = s_sqr.reserve()

            def v_xn(v, t=t, slot=slot, sqv=sqv, xiv=xiv, sct_prev=sct_prev):
                xt = xbufs[slot]
                xb = nbufs[slot]
                v.wait_ge(s_sqr.h, sqv)
                v.wait_ge(s_xin[slot].h, xiv)
                if t >= NS:
                    v.wait_ge(s_tr.h, 8 * (t - NS + 1))
                    v.wait_ge(s_sct[t % 4].h, sct_prev)
                v.tensor_scalar(xb, xt, negm[:, t:t + 1], rstd[:, t:t + 1],
                                OP.add, OP.mult).then_inc(s_xn.h, 1)
            VX.append(v_xn)
            TS[t]["xnv"] = s_xn.reserve()
            if t + NS < NTILE:
                xin_val[t + NS] = y_xload(t + NS, (t + NS) % NS, TS[t]["xnv"])

        EV_ENG = ["v", "s", "v", "v", "v", "s", "v", "v"]

        def st_EF(t):
            slot = t % NS
            xnv = TS[t]["xnv"]
            TS[t]["ev"] = {}
            for k in range(KCH):
                psl = k % PTR_SLOTS
                ek = EV_ENG[k]
                kb = k - PTR_SLOTS
                if kb >= 0:
                    tdv = TS[t]["ev"][kb]
                elif t > 0:
                    tdv = TS[t - 1]["ev"][kb + KCH]
                else:
                    tdv = None

                def t_tr(e, t=t, k=k, slot=slot, psl=psl, xnv=xnv, tdv=tdv):
                    e.wait_ge(s_c.h, 160)
                    e.wait_ge(s_xn.h, xnv)
                    if tdv:
                        e.wait_ge(s_trv[tdv[0]].h, tdv[1])
                    e.transpose(ptr_sl[psl],
                                nbufs[slot][:, k * 128:(k + 1) * 128],
                                id128[:, :]).then_inc(s_tr.h, 1)
                TX.append(t_tr)
                trv = s_tr.reserve()

                dst_sl = slice(k * TOK + t * 128, k * TOK + t * 128 + 128)
                if ek == "s":
                    def c_trev(s, psl=psl, trv=trv, dst_sl=dst_sl):
                        s.wait_ge(s_tr.h, trv)
                        s.copy(bigX[:, dst_sl], ptr_sl[psl]).then_inc(s_trv["s"].h, 1)
                    SX.append(c_trev)
                elif ek == "v":
                    def v_trev(v, psl=psl, trv=trv, dst_sl=dst_sl):
                        v.wait_ge(s_tr.h, trv)
                        v.tensor_copy(bigX[:, dst_sl], ptr_sl[psl]).then_inc(s_trv["v"].h, 1)
                    VX.append(v_trev)
                else:
                    def g_trev(g, psl=psl, trv=trv, dst_sl=dst_sl):
                        g.wait_ge(s_tr.h, trv)
                        g.tensor_copy(bigX[:, dst_sl], ptr_sl[psl]).then_inc(s_trv["g"].h, 1)
                    GX.append(g_trev)
                TS[t]["ev"][k] = (ek, s_trv[ek].reserve())

        def st_G(t):
            bank = t % 2
            vdl = vg_done_log[bank]
            trv_now = {ek: s_trv[ek].val for ek in ("s", "v", "g")}

            def t_logit(e, t=t, bank=bank, vdl=vdl, trv_now=dict(trv_now)):
                for ek, tv in trv_now.items():
                    if tv:
                        e.wait_ge(s_trv[ek].h, tv)
                if vdl:
                    e.wait_ge(s_vg.h, vdl[0])
                    e.wait_ge(s_sg.h, vdl[1])
                for k in range(KCH):
                    inst = e.matmul(plog[bank][:, :],
                                    bigX[:, k * TOK + t * 128: k * TOK + t * 128 + 128],
                                    wgT[:, k * 8:(k + 1) * 8],
                                    start=(k == 0), stop=(k == KCH - 1))
                inst.then_inc(s_mmg.h, 1)
            TX.append(t_logit)
            TS[t]["mgv"] = s_mmg.reserve()

        def st_HIJ(t):
            bank = t % 2
            mgv = TS[t]["mgv"]

            def v_gate(v, t=t, bank=bank, mgv=mgv):
                v.wait_ge(s_mmg.h, mgv)
                cover(v, "v")
                v.reduce_max(ngmx[:, t:t + 1], plog[bank][:, :], axis=AX, negate=True)
                edge(v, "v", v.reduce_max(gmx[:, t:t + 1], plog[bank][:, :], axis=AX))
                edge(v, "v", v.tensor_scalar(tmp8[:, :], plog[bank][:, :],
                                             gmx[:, t:t + 1], None, OP.is_equal))
                edge(v, "v", v.tensor_tensor_scan(tmp8b[:, :], tmp8[:, :], tmp8[:, :],
                                                  0.0, OP.add, OP.bypass))
                edge(v, "v", v.tensor_scalar(tmp8b[:, :], tmp8b[:, :], 1.0, None,
                                             OP.is_equal))
                v.tensor_mul(maskb[:, t * 8:(t + 1) * 8], tmp8[:, :], tmp8b[:, :]
                             ).then_inc(s_vg.h, 1)
            VX.append(v_gate)
            vgm = s_vg.reserve()

            def c_gexp(s, t=t, bank=bank, vgm=vgm):
                s.wait_ge(s_vg.h, vgm)
                s.activation(gexp[:, bank * 8:(bank + 1) * 8], plog[bank][:, :],
                             AF.Exp, bias=ngmx[:, t:t + 1], scale=1.0,
                             accum_out=sexp[:, t:t + 1]).then_inc(s_sg.h, 1)
            SX.append(c_gexp)
            sgv = s_sg.reserve()
            vg_done_log[bank] = (vgm, sgv)

            vdp = vg_done_pref[bank]
            prefv = TS[t - 1]["prefv"] if t > 0 else 0

            def t_pref(e, t=t, bank=bank, vgm=vgm, vdp=vdp, prefv=prefv):
                e.wait_ge(s_c.h, 160)
                e.wait_ge(s_eps.h, 1)
                e.wait_ge(s_vg.h, vgm)
                if prefv:
                    e.wait_ge(s_vg.h, prefv)
                if vdp:
                    e.wait_ge(s_vg.h, vdp)
                e.matmul(ppref[bank][:, :], lstrict[:, :], maskb[:, t * 8:(t + 1) * 8],
                         start=True, stop=False)
                e.matmul(ppref[bank][:, :], ones1[:, :], cntrun[:, :],
                         start=False, stop=True)
                e.matmul(pcnt0[:, :], allone[:, 0:1], maskb[:, t * 8:(t + 1) * 8],
                         start=True, stop=True).then_inc(s_mmp.h, 1)
            TX.append(t_pref)
            TS[t]["mpv"] = s_mmp.reserve()
            TS[t]["sgv"] = sgv

        def st_K(t):
            bank = t % 2
            slot = t % NS
            mpv, sgv = TS[t]["mpv"], TS[t]["sgv"]

            mulv = s_vg.val + 1

            def v_pref(v, t=t, bank=bank, mpv=mpv, sgv=sgv, mulv=mulv):
                v.wait_ge(s_mmp.h, mpv)
                v.wait_ge(s_sg.h, sgv)
                cover(v, "v")
                v.reciprocal(g1[:, t:t + 1], sexp[:, t:t + 1])
                v.tensor_mul(tmp8[:, :], maskb[:, t * 8:(t + 1) * 8],
                             ppref[bank][:, :]).then_inc(s_vg.h, 1)
                v.wait_ge(s_vg.h, mulv)
                edge(v, "v", v.reduce_sum(rankb[:, t:t + 1], tmp8[:, :], axis=AX))
                v.tensor_add(cntrun[:, :], cntrun[:, :],
                             pcnt0[:, :]).then_inc(s_vg.h, 1)
            VX.append(v_pref)
            s_vg.reserve()
            vg_done_pref[bank] = s_vg.val
            TS[t]["prefv"] = s_vg.reserve()

            def v_idx(v, t=t, bank=bank):
                v.wait_ge(s_ci.h, 1)
                cover(v, "v")
                mk = maskb[:, t * 8:(t + 1) * 8]
                edge(v, "v", v.scalar_tensor_tensor(tmp8b[:, :], mk, 1.0, iota8,
                                                    OP.mult, OP.mult,
                                                    accum_out=tmp1[:, 1:2]))
                v.tensor_scalar(tmp1[:, 3:4], rankb[:, t:t + 1], float(CLOC), None, OP.is_lt)
                edge(v, "v", v.tensor_scalar(tmp1[:, 2:3], tmp1[:, 1:2], float(CLOC),
                                             rankb[:, t:t + 1], OP.mult, OP.add))
                edge(v, "v", v.tensor_scalar(tmp1[:, 2:3], tmp1[:, 2:3], -float(NSLOT),
                                             None, OP.add))
                edge(v, "v", v.tensor_scalar(tmp1[:, 2:3], tmp1[:, 2:3], tmp1[:, 3:4],
                                             float(NSLOT), OP.mult, OP.add))
                v.tensor_copy(sloti[:, t:t + 1], tmp1[:, 2:3]).then_inc(s_idx.h, 1)
            VX.append(v_idx)
            s_idx.reserve()
            idxv = s_idx.val

            def g_scat(g, t=t, slot=slot, idxv=idxv, zfv=zfv):
                g.wait_ge(s_z.h, zfv)
                g.wait_ge(s_idx.h, idxv)
                g.indirect_dma_start(
                    out=xg_h, out_offset=IndirectOffsetOnAxis(sloti[:, t:t + 1], 0),
                    in_=nbufs[slot], in_offset=None,
                ).then_inc(s_sct[t % 4].h, 16)
            GX.append(g_scat)
            sct_of_tile[t + NS] = s_sct[t % 4].reserve(16)

        for p in range(NTILE + 4):
            if 1 <= p < NTILE + 1:
                st_CD(p - 1)
            if p >= 4:
                st_K(p - 4)
            if 3 <= p < NTILE + 3:
                st_G(p - 3)
            if 2 <= p < NTILE + 2:
                st_EF(p - 2)
            if 3 <= p < NTILE + 3:
                st_HIJ(p - 3)
            if p < NTILE:
                st_AB(p)

        pbc8 = ppref[0][0:8, 0:8]
        vgv_all = s_vg.val
        idx_all = s_idx.val
        def t_bc8(e, vgv_all=vgv_all, idx_all=idx_all):
            e.wait_ge(s_c.h, 160)
            e.wait_ge(s_vg.h, vgv_all)
            e.wait_ge(s_idx.h, idx_all)
            e.matmul(pbc8, ones1[:, 0:8], cntrun[:, :], start=True,
                     stop=True).then_inc(s_mmp.h, 1)
        TX.append(t_bc8)
        mpv = s_mmp.reserve()

        def v_ccin(v, mpv=mpv):
            v.wait_ge(s_mmp.h, mpv)
            v.tensor_scalar(ccin_s[:, :], pbc8, myhot[:, :], None,
                            OP.mult).then_inc(s_vg.h, 1)
        VX.append(v_ccin)
        ccin_ready = s_vg.reserve()
        ccv = s_c.reserve(16)
        s_cc.reserve(1)
        s_ccd.reserve(16)

        def g_cc(g, ccin_ready=ccin_ready, ccv=ccv):
            g.wait_ge(s_vg.h, ccin_ready)
            g.dma_start(out=cc_in.ap(), in_=ccin_s[:, :]).then_inc(s_c.h, 16)
            g.wait_ge(s_c.h, ccv)
            g.collective_compute(
                "AllReduce", OP.add,
                replica_groups=[[0]] if SIM_SINGLE else [list(range(N_CORES))],
                ins=[cc_in.ap()],
                outs=[cc_out.ap()],
            ).then_inc(s_cc.h, 1)
            g.wait_ge(s_cc.h, 1)
            g.dma_start(out=cntmat[:, :], in_=cc_out.ap()).then_inc(s_ccd.h, 16)
        sctv = [s_sct[i].val for i in range(4)]

        def g_sctwait(g, sctv=tuple(sctv)):
            for i in range(4):
                g.wait_ge(s_sct[i].h, sctv[i])
        GX.append(g_sctwait)

        ps1.close()
        barrier()
        GX.append(g_cc)

        def y_xgt67(y):
            for k in (6, 7):
                y.dma_start(out=bigX[:, k * NSLOT:(k + 1) * NSLOT],
                            in_=xg_h[0:NSLOT, k * 128:(k + 1) * 128],
                            transpose=True).then_inc(s_tr2.h, 16)
        YX.append(y_xgt67)
        s_tr2.reserve(2 * 16)

        va_qk = [0, 0]
        va_v = [0, 0]
        sa_done = [0]
        qks = [(hbuf[0:64, 0:2048], hbuf[0:64, 2048:4096]), (qk1[0][:, :], qk1[1][:, :])]
        vbufs = [gbufs, v1buf]
        with ExitStack() as psA:
            pqk = [psA.enter_context(nc.psum_tensor(f"pqk{i}", [64, 512], F32)) for i in range(2)]
            pv = [psA.enter_context(nc.psum_tensor(f"pv{i}", [128, 64], F32)) for i in range(2)]
            for b in range(2):
                base = b * 2048
                for qk in range(2):
                    for i4 in range(4):
                        bank = (qk * 4 + i4) % 2
                        vqv = va_qk[bank]

                        def t_qk(e, qk=qk, i4=i4, bank=bank, base=base, vqv=vqv):
                            e.wait_ge(s_c.h, 160)
                            if vqv:
                                e.wait_ge(s_va.h, vqv)
                            for k in range(KCH):
                                inst = e.matmul(pqk[bank][:, :],
                                                wqkvT[:, qk * 512 + k * 64: qk * 512 + (k + 1) * 64],
                                                bigX[:, k * TOK + base + i4 * 512: k * TOK + base + i4 * 512 + 512],
                                                start=(k == 0), stop=(k == KCH - 1))
                            inst.then_inc(s_mma.h, 1)
                        TX.append(t_qk)
                        mav = s_mma.reserve()

                        def v_qk(v, b=b, qk=qk, i4=i4, bank=bank, mav=mav):
                            v.wait_ge(s_mma.h, mav)
                            dst = qks[b][qk][:, i4 * 512:(i4 + 1) * 512]
                            v.tensor_copy(dst, pqk[bank][:, :]).then_inc(s_va.h, 1)
                        VX.append(v_qk)
                        va_qk[bank] = s_va.reserve()
                for it in range(16):
                    bank = it % 2
                    vvv = va_v[bank]

                    def t_v(e, it=it, bank=bank, base=base, vvv=vvv):
                        e.wait_ge(s_c.h, 160)
                        if vvv:
                            e.wait_ge(s_va.h, vvv)
                        for k in range(KCH):
                            inst = e.matmul(pv[bank][:, :],
                                            bigX[:, k * TOK + base + it * 128: k * TOK + base + it * 128 + 128],
                                            wqkvT[:, 1024 + k * 64: 1024 + (k + 1) * 64],
                                            start=(k == 0), stop=(k == KCH - 1))
                        inst.then_inc(s_mma.h, 1)
                    TX.append(t_v)
                    mav = s_mma.reserve()

                    def v_v(v, b=b, it=it, bank=bank, mav=mav):
                        v.wait_ge(s_mma.h, mav)
                        v.tensor_copy(vbufs[b][:, it * 64:(it + 1) * 64],
                                      pv[bank][:, :]).then_inc(s_va.h, 1)
                    VX.append(v_v)
                    va_v[bank] = s_va.reserve()

        qkv_done = s_mma.val

        def y_xgt05(y, qkv_done=qkv_done):
            y.wait_ge(s_mma.h, qkv_done)
            for k in range(6):
                y.dma_start(out=bigX[:, k * NSLOT:(k + 1) * NSLOT],
                            in_=xg_h[0:NSLOT, k * 128:(k + 1) * 128],
                            transpose=True).then_inc(s_tr2.h, 16)
        YX.append(y_xgt05)
        s_tr2.reserve(6 * 16)

        for b in range(2):
            base = b * 2048
            with ExitStack() as psB:
                pscore = psB.enter_context(nc.psum_tensor(f"pscore{b}", [128, 2048], F32))
                pao = [psB.enter_context(nc.psum_tensor(f"pao{b}{i}", [64, 512], F32)) for i in range(4)]
                qk_done = s_va.val
                vpcol = 4096 + b * 1024
                for kt in range(16):
                    ebank = kt % 2
                    sdv = sa_done[0]
                    mma_prev = s_mma.val

                    def t_sc(e, b=b, kt=kt, sdv=sdv, qk_done=qk_done, pscore=pscore):
                        e.wait_ge(s_va.h, qk_done)
                        if sdv:
                            e.wait_ge(s_sa.h, sdv)
                        for i4 in range(4):
                            e.matmul(pscore[:, i4 * 512:(i4 + 1) * 512],
                                     qks[b][1][:, kt * 128:(kt + 1) * 128],
                                     qks[b][0][:, i4 * 512:(i4 + 1) * 512],
                                     start=True, stop=True).then_inc(s_mma.h, 1)
                    TX.append(t_sc)
                    mavs = [s_mma.reserve() for _ in range(4)]

                    def v_max(v, kt=kt, mavs=tuple(mavs), pscore=pscore):
                        for i4 in range(4):
                            v.wait_ge(s_mma.h, mavs[i4])
                            v.reduce_max(tmp1[:, i4:i4 + 1],
                                         pscore[:, i4 * 512:(i4 + 1) * 512], axis=AX)
                        flow(v, "v")
                        v.reduce_max(ngmx[:, kt:kt + 1], tmp1[:, 0:4], axis=AX,
                                     negate=True).then_inc(s_va.h, 1)
                    VX.append(v_max)
                    vmx = s_va.reserve()

                    def c_exp(s, kt=kt, ebank=ebank, vmx=vmx, mma_prev=mma_prev, pscore=pscore,
                              zfv=zfv):
                        s.wait_ge(s_z.h, zfv)
                        s.wait_ge(s_zg.h, 16)
                        s.wait_ge(s_va.h, vmx)
                        if kt >= 2:
                            s.wait_ge(s_mma.h, mma_prev)
                        s.activation(etile[ebank][:, :], pscore[:, :],
                                     AF.Exp, bias=ngmx[:, kt:kt + 1], scale=1.0,
                                     accum_out=den[:, kt:kt + 1]).then_inc(s_sa.h, 1)
                    SX.append(c_exp)
                    sa_done[0] = s_sa.reserve()
                    sav = s_sa.val

                    def v_vp(v, b=b, kt=kt, sav=sav, vpcol=vpcol):
                        v.wait_ge(s_sa.h, sav)
                        flow(v, "v")
                        edge(v, "v", v.reciprocal(rden[:, kt:kt + 1], den[:, kt:kt + 1]))
                        v.tensor_scalar(hbuf[:, vpcol + kt * 64: vpcol + (kt + 1) * 64],
                                        vbufs[b][:, kt * 64:(kt + 1) * 64],
                                        rden[:, kt:kt + 1], None, OP.mult
                                        ).then_inc(s_va.h, 1)
                    VX.append(v_vp)
                    vpv = s_va.reserve()

                    for i4 in range(4):
                        def t_ao(e, b=b, kt=kt, i4=i4, ebank=ebank, vpv=vpv, pao=pao,
                                 vpcol=vpcol):
                            e.wait_ge(s_va.h, vpv)
                            e.matmul(pao[i4][:, :],
                                     hbuf[:, vpcol + kt * 64: vpcol + (kt + 1) * 64],
                                     etile[ebank][:, i4 * 512:(i4 + 1) * 512],
                                     start=(kt == 0), stop=(kt == 15)
                                     ).then_inc(s_mma.h, 1)
                        TX.append(t_ao)
                        s_mma.reserve()
                mma_fin = s_mma.val

                def v_ao(v, base=base, mma_fin=mma_fin, pao=pao):
                    v.wait_ge(s_mma.h, mma_fin)
                    for i4 in range(4):
                        v.tensor_copy(attnF[:, base + i4 * 512: base + i4 * 512 + 512],
                                      pao[i4][:, :])
                    v.drain().then_inc(s_va.h, 1)
                VX.append(v_ao)
                s_va.reserve()

        barrier()

        def record_p2b(ps_scope):
            poff = ps_scope.enter_context(nc.psum_tensor("poff", [1, 8], F32))
            pobc2 = ps_scope.enter_context(nc.psum_tensor("pobc2", [128, 8], F32))
            ccdv = s_ccd.val

            def t_off(e, ccdv=ccdv):
                e.wait_ge(s_c.h, 160)
                e.wait_ge(s_ccd.h, ccdv)
                e.matmul(poff[:, :], premask[:, :], cntmat[:, :], start=True,
                         stop=True).then_inc(s_mmp.h, 1)
            TX.append(t_off)
            mpv = s_mmp.reserve()

            def v_poke(v, mpv=mpv):
                v.wait_ge(s_mmp.h, mpv)
                flow(v, "v")
                v.tensor_copy(tmp8[0:1, :], poff[:, :]).then_inc(s_vg.h, 1)
            VX.append(v_poke)
            offrow_ready = s_vg.reserve()

            def t_obc2(e, offrow_ready=offrow_ready):
                e.wait_ge(s_vg.h, offrow_ready)
                e.matmul(pobc2[:, :], ones1[:, :], tmp8[0:1, :], start=True,
                         stop=True).then_inc(s_mmp.h, 1)
            TX.append(t_obc2)
            mpv = s_mmp.reserve()

            oaw = (s_oac[0].val, s_oac[1].val, s_oacy[0].val, s_oacy[1].val)

            def v_keep(v, mpv=mpv, oaw=oaw):
                v.wait_ge(s_mmp.h, mpv)
                v.wait_ge(s_oac[0].h, oaw[0])
                v.wait_ge(s_oac[1].h, oaw[1])
                v.wait_ge(s_oacy[0].h, oaw[2])
                v.wait_ge(s_oacy[1].h, oaw[3])
                flow(v, "v")
                for t in range(NTILE):
                    v.scalar_tensor_tensor(stg[0][:, t * 8:(t + 1) * 8],
                                           maskb[:, t * 8:(t + 1) * 8], 1.0,
                                           pobc2[:, :], OP.mult, OP.mult,
                                           accum_out=posb[:, t:t + 1])
                flow(v, "v")
                v.tensor_add(posb[:, :], posb[:, :], rankb[:, :])
                flow(v, "v")
                v.tensor_scalar(keepb[:, :], posb[:, :], CAP, None, OP.is_lt)
                flow(v, "v")
                v.tensor_mul(coefb[:, :], keepb[:, :], g1[:, :])
                v.drain().then_inc(s_vg.h, 1)
            VX.append(v_keep)
            return s_vg.reserve()

        xgt_a = s_tr2.val

        def t_xgtwait(e, xgt_a=xgt_a):
            e.wait_ge(s_tr2.h, xgt_a)
        TX.append(t_xgtwait)
        barrier()

        ps4 = ExitStack()
        ph = [ps4.enter_context(nc.psum_tensor(f"ph{i}", [128, 512], F32)) for i in range(2)]
        po = [ps4.enter_context(nc.psum_tensor(f"po{i}", [128, 512], F32)) for i in range(4)]
        relu_done = [0, 0]
        oev_done = [0, 0, 0, 0]
        stg_last = [None, None]
        oac_region = {}
        CHUNKS = [(0, 512), (512, 256)]

        wv1_next = [s_w.val]
        for pi in range(NEXP * NDQ):
            e_, dq = divmod(pi, NDQ)
            wv1 = wv1_next[0]
            wv = s_w.val
            wbuf = w1b[pi % 2]
            mo_prev = s_mmo.val
            for (coff, cw) in CHUNKS:
                for fd in range(8):
                    bank = fd % 2
                    rdv = relu_done[bank]

                    def t_h(e, e_=e_, coff=coff, cw=cw, fd=fd, bank=bank, wv1=wv1,
                            rdv=rdv, wbuf=wbuf):
                        e.wait_ge(s_w.h, wv1)
                        if rdv:
                            e.wait_ge(s_relu.h, rdv)
                        for k in range(KCH):
                            inst = e.matmul(
                                ph[bank][:, 0:cw],
                                wbuf[:, k * DQ + fd * 128: k * DQ + fd * 128 + 128],
                                bigX[:, k * NSLOT + e_ * CLOC + coff: k * NSLOT + e_ * CLOC + coff + cw],
                                start=(k == 0), stop=(k == KCH - 1))
                        inst.then_inc(s_mmh.h, 1)
                    TX.append(t_h)
                    mhv = s_mmh.reserve()

                    def c_relu(s, e_=e_, dq=dq, coff=coff, cw=cw, fd=fd, bank=bank,
                               mhv=mhv, mo_prev=mo_prev, pi=pi):
                        if mo_prev:
                            s.wait_ge(s_mmo.h, mo_prev)
                        s.wait_ge(s_mmh.h, mhv)
                        s.activation(hbuf[:, fd * CLOC + coff: fd * CLOC + coff + cw],
                                     ph[bank][:, 0:cw], AF.Relu,
                                     bias=b1T[:, (pi % 2) * 8 + fd: (pi % 2) * 8 + fd + 1],
                                     scale=1.0).then_inc(s_relu.h, 1)
                    SX.append(c_relu)
                    relu_done[bank] = s_relu.reserve()
            if pi + 1 < NEXP * NDQ:
                ne, ndq = divmod(pi + 1, NDQ)
                mh_cur = s_mmh.val
                wv_prev = s_w.val
                nbuf = w1b[(pi + 1) % 2]
                mh_free = mh_cur - 16

                def g_pref1(g, ne=ne, ndq=ndq, mh_free=mh_free, nbuf=nbuf, pi=pi,
                            wv_prev=wv_prev):
                    g.wait_ge(s_w.h, wv_prev)
                    if mh_free > 0:
                        g.wait_ge(s_mmh.h, mh_free)
                    g.dma_start(out=b1T[:, ((pi + 1) % 2) * 8: ((pi + 1) % 2) * 8 + 8],
                                in_=b1T_h[:, ne * 32 + ndq * 8: ne * 32 + ndq * 8 + 8]
                                ).then_inc(s_w.h, 16)
                    g.dma_start(out=nbuf[:, :], in_=w1_h[ne, ndq]).then_inc(s_w.h, 16)
                GX.append(g_pref1)
                s_w.reserve(32)
                wv1_next[0] = s_w.val
            rall = s_relu.val
            for st_ in range(6):
                stgbuf = stg[st_ % 2]
                oacv = stg_last[st_ % 2]
                for nf in range(2):
                    bank = (st_ * 2 + nf) % 4
                    odv = oev_done[bank]

                    def t_o(e, dq=dq, st_=st_, nf=nf, bank=bank, rall=rall, odv=odv, wv=wv):
                        e.wait_ge(s_w.h, wv)
                        e.wait_ge(s_relu.h, rall)
                        if odv:
                            e.wait_ge(s_oev.h, odv)
                        for fd in range(8):
                            last = (fd == 7) and (dq != 0)
                            inst = e.matmul(po[bank][:, :],
                                            hbuf[:, fd * CLOC + st_ * 128: fd * CLOC + st_ * 128 + 128],
                                            w2b[:, fd * D + nf * 512: fd * D + nf * 512 + 512],
                                            start=(fd == 0), stop=last)
                        if dq == 0:
                            inst = e.matmul(po[bank][:, :], allone[0:1, :],
                                            b2b[:, nf * 512:(nf + 1) * 512],
                                            start=False, stop=True)
                        inst.then_inc(s_mmo.h, 1)
                    TX.append(t_o)
                    mov = s_mmo.reserve()

                    def v_oev(v, st_=st_, nf=nf, bank=bank, mov=mov, oacv=oacv, stgbuf=stgbuf):
                        v.wait_ge(s_mmo.h, mov)
                        if oacv is not None:
                            v.wait_ge(oacv[0].h, oacv[1])
                        v.tensor_copy(stgbuf[:, nf * 512:(nf + 1) * 512],
                                      po[bank][:, :]).then_inc(s_oev.h, 1)
                    VX.append(v_oev)
                    oev_done[bank] = s_oev.reserve()
                oev_now = s_oev.val
                row0 = e_ * CLOC + st_ * 128
                reg_prev = oac_region.get(st_) if dq != 0 else None
                gw_prev = s_oac[st_ % 2].val

                if dq == 0:
                    def y_oac(y, row0=row0, oev_now=oev_now, stgbuf=stgbuf, st_=st_):
                        y.wait_ge(s_oev.h, oev_now)
                        y.dma_start(out=og_h[row0:row0 + 128, :], in_=stgbuf[:, :]
                                    ).then_inc(s_oacy[st_ % 2].h, 16)
                    YX.append(y_oac)
                    oac_region[st_] = (s_oacy[st_ % 2], s_oacy[st_ % 2].reserve(16))
                    stg_last[st_ % 2] = oac_region[st_]
                else:
                    def g_oac(g, dq=dq, row0=row0, oev_now=oev_now, stgbuf=stgbuf, st_=st_,
                              reg_prev=reg_prev, gw_prev=gw_prev):
                        g.wait_ge(s_oev.h, oev_now)
                        if gw_prev:
                            g.wait_ge(s_oac[st_ % 2].h, gw_prev)
                        if reg_prev is not None:
                            g.wait_ge(reg_prev[0].h, reg_prev[1])
                        g.dma_start(out=og_h[row0:row0 + 128, :], in_=stgbuf[:, :],
                                    accum_op=OP.add).then_inc(s_oac[st_ % 2].h, 16)
                    GX.append(g_oac)
                    oac_region[st_] = (s_oac[st_ % 2], s_oac[st_ % 2].reserve(16))
                    stg_last[st_ % 2] = oac_region[st_]
            if pi + 1 < NEXP * NDQ:
                ne, ndq = divmod(pi + 1, NDQ)
                mo_cur = s_mmo.val
                wv_prev2 = s_w.val

                def g_pref2(g, ne=ne, ndq=ndq, mo_cur=mo_cur, wv_prev2=wv_prev2):
                    g.wait_ge(s_w.h, wv_prev2)
                    g.wait_ge(s_mmo.h, mo_cur)
                    g.dma_start(out=w2b[:, :], in_=w2_h[ne, ndq]).then_inc(s_w.h, 16)
                    if ndq == 0:
                        g.dma_start(out=b2b[:, :], in_=b2_h[ne:ne + 1, :]).then_inc(s_w.h, 16)
                GX.append(g_pref2)
                s_w.reserve(16 * (1 + (1 if ndq == 0 else 0)))

        coef_ready = record_p2b(ps4)

        oa0, oa1 = s_oac[0].val, s_oac[1].val
        oy0, oy1 = s_oacy[0].val, s_oacy[1].val

        def g_oacwait(g, oa0=oa0, oa1=oa1, oy0=oy0, oy1=oy1):
            g.wait_ge(s_oac[0].h, oa0)
            g.wait_ge(s_oac[1].h, oa1)
            g.wait_ge(s_oacy[0].h, oy0)
            g.wait_ge(s_oacy[1].h, oy1)
        GX.append(g_oacwait)

        ps4.close()
        barrier()

        ps5 = ExitStack()
        prep = [ps5.enter_context(nc.psum_tensor(f"prep{i}", [128, 512], F32)) for i in range(2)]
        gb5 = [gbufs[:, :], gbuf2[:, :], gbuf3[:, :], gbuf4[:, :]]
        ob5 = [stg[0][:, :], stg[1][:, :], ob3[:, :], ob4[:, :]]
        rep_evac = [0, 0]
        out_free = [0, 0, 0, 0]
        gb_free = [0, 0, 0, 0]
        sct5 = [0, 0, 0, 0]
        xin2 = [0] * (NTILE + NS)
        for _t in range(NS):
            xin2[_t] = y_xload(_t, _t, 0)

        for t in range(NTILE):
            slot = t % NS
            par = t % 2
            xiv = xin2[t]
            gfv = gb_free[slot]

            def g_gath(g, t=t, slot=slot, gfv=gfv):
                g.wait_ge(s_zg.h, 16)
                if gfv:
                    g.wait_ge(s_out.h, gfv)
                g.indirect_dma_start(
                    out=gb5[slot], out_offset=None,
                    in_=og_h, in_offset=IndirectOffsetOnAxis(sloti[:, t:t + 1], 0),
                ).then_inc(s_og[slot].h, 16)
            GX.append(g_gath)
            ogv = s_og[slot].reserve(16)

            rep_need = s_rep.val - 2 * (NS - 1) if t >= NS else 0

            def c_cast(s, t=t, slot=slot, xiv=xiv, rep_need=rep_need):
                s.wait_ge(s_xin[slot].h, xiv)
                if rep_need > 0:
                    s.wait_ge(s_rep.h, rep_need)
                s.copy(nbufs[slot], xbufs[slot]).then_inc(s_xn.h, 1)
            SX.append(c_cast)
            xcv = s_xn.reserve()

            for j in range(2):
                bank = j
                rev = rep_evac[bank]

                def t_rep(e, t=t, j=j, slot=slot, bank=bank, rev=rev, xcv=xcv):
                    e.wait_ge(s_c.h, 160)
                    e.wait_ge(s_xn.h, xcv)
                    if rev:
                        e.wait_ge(s_out.h, rev)
                    e.matmul(prep[bank][:, :], attnF[:, t * 128:(t + 1) * 128],
                             rep64[:, j * 512:(j + 1) * 512],
                             start=True, stop=False)
                    e.matmul(prep[bank][:, :], id128[:, :],
                             nbufs[slot][:, j * 512:(j + 1) * 512],
                             start=False, stop=True).then_inc(s_rep.h, 1)
                TX.append(t_rep)
                rpv = s_rep.reserve()
                ofv = out_free[slot] if t >= NS and j == 0 else 0

                def v_out(v, t=t, j=j, slot=slot, bank=bank, rpv=rpv,
                          ofv=ofv, ogv=ogv):
                    dst = ob5[slot][:, j * 512:(j + 1) * 512]
                    v.wait_ge(s_vg.h, coef_ready)
                    v.wait_ge(s_rep.h, rpv)
                    v.wait_ge(s_og[slot].h, ogv)
                    if ofv:
                        v.wait_ge(s_st[slot].h, ofv)
                    v.scalar_tensor_tensor(dst, gb5[slot][:, j * 512:(j + 1) * 512],
                                           coefb[:, t:t + 1], prep[bank][:, :],
                                           OP.mult, OP.add).then_inc(s_out.h, 1)
                VX.append(v_out)
                rep_evac[bank] = s_out.reserve()
            gb_free[slot] = s_out.val
            ov = s_out.val

            if t % 2 == 0:
                def y_store(y, t=t, slot=slot, ov=ov):
                    y.wait_ge(s_out.h, ov)
                    y.dma_start(out=y_h[t * 128:(t + 1) * 128, :], in_=ob5[slot]
                                ).then_inc(s_st[slot].h, 16)
                YX.append(y_store)
            else:
                def c_store(s, t=t, slot=slot, ov=ov):
                    s.wait_ge(s_out.h, ov)
                    s.dma_start(out=y_h[t * 128:(t + 1) * 128, :], in_=ob5[slot]
                                ).then_inc(s_st[slot].h, 16)
                SX.append(c_store)
            out_free[slot] = s_st[slot].reserve(16)
            if t + NS < NTILE:
                xin2[t + NS] = y_xload(t + NS, (t + NS) % NS, xcv)

        stvals = [s_st[i].val for i in range(4)]

        def y_final(y, stvals=tuple(stvals)):
            for i in range(4):
                y.wait_ge(s_st[i].h, stvals[i])
        YX.append(y_final)
        ps5.close()

        block = stack.enter_context(nc.Block())

        @block.tensor
        def _(e):
            for f in TX:
                f(e)

        @block.vector
        def _(v):
            for f in VX:
                f(v)

        @block.scalar
        def _(s):
            for f in SX:
                f(s)

        @block.gpsimd
        def _(g):
            for f in GX:
                f(g)

        @block.sync
        def _(y):
            for f in YX:
                f(y)

    return nc


_NC_CACHE = None


def kernel(L0_attn_x, L0_attn_Wq, L0_attn_Wk, L0_attn_Wv, L0_ffn_wg,
           L0_ffn_w1, L0_ffn_b1, L0_ffn_w2, L0_ffn_b2):
    global _NC_CACHE
    x = np.ascontiguousarray(np.asarray(L0_attn_x, dtype=np.float32))
    Wq = np.asarray(L0_attn_Wq, np.float32)
    Wk = np.asarray(L0_attn_Wk, np.float32)
    Wv = np.asarray(L0_attn_Wv, np.float32)
    wg = np.asarray(L0_ffn_wg, np.float32)
    w1 = np.asarray(L0_ffn_w1, np.float32)
    b1 = np.asarray(L0_ffn_b1, np.float32)
    w2 = np.asarray(L0_ffn_w2, np.float32)
    b2 = np.ascontiguousarray(np.asarray(L0_ffn_b2, np.float32))

    def t_small(W):
        return np.ascontiguousarray(W.reshape(64, 8, 128).transpose(2, 1, 0))
    wqkvT = np.ascontiguousarray(np.concatenate(
        [t_small(Wq).reshape(128, 512), t_small(Wk).reshape(128, 512),
         t_small(Wv).reshape(128, 512)], axis=1))
    wgT = np.ascontiguousarray(wg.reshape(8, 128, 8).transpose(1, 0, 2).reshape(128, 64))
    b1T = np.ascontiguousarray(b1.reshape(8, 32, 128).transpose(2, 0, 1).reshape(128, 256))
    w1h = np.ascontiguousarray(w1.reshape(8, 8, 128, 4, 1024).transpose(0, 3, 2, 1, 4))
    w2h = np.ascontiguousarray(w2.reshape(8, 4, 8, 128, 1024).transpose(0, 1, 3, 2, 4))

    id128 = np.eye(128, dtype=np.float32)
    lstrict = np.triu(np.ones((128, 128), np.float32), k=1)
    allone = np.ones((128, 128), np.float32)
    rep64 = (np.arange(1024)[None, :] % 64 == np.arange(64)[:, None]).astype(np.float32)

    if _NC_CACHE is None:
        _NC_CACHE = build_program()
    nc = _NC_CACHE

    in_maps = []
    for c in range(N_CORES):
        pre = np.zeros((8, 1), np.float32)
        pre[:c, 0] = 1.0
        hot = np.zeros((8, 1), np.float32)
        hot[c, 0] = 1.0
        in_maps.append({
            "x": x[2 * c:2 * c + 2].reshape(TOK, D),
            "wqkvT": wqkvT, "wgT": wgT, "b1T": b1T, "b2": b2,
            "w1": w1h, "w2": w2h,
            "id128": id128, "lstrict": lstrict, "allone": allone, "rep64": rep64,
            "premask": pre, "myhot": hot,
        })
    res = run_bass_kernel_spmd(nc, in_maps, core_ids=list(range(N_CORES)))
    out = np.concatenate([r["y"].reshape(2, S, D) for r in res.results], axis=0)
    return out


# version 24 (final)
# speedup vs baseline: 1.0892x; 1.0892x over previous
import numpy as np
import ml_dtypes
import concourse.bass as bass
import concourse.mybir as mybir
from concourse.bass_utils import run_bass_kernel_spmd
from concourse.bass_core_rust import IndirectOffsetOnAxis
from contextlib import ExitStack

F32 = mybir.dt.float32
BF16 = mybir.dt.bfloat16
I32 = mybir.dt.int32
OP = mybir.AluOpType
AF = mybir.ActivationFunctionType
AX = mybir.AxisListType.X

N_CORES = 8
B, S, D = 16, 2048, 1024
TOK = 4096
NTILE = 32
KCH = 8
NEXP, DFF, NDQ, DQ = 8, 4096, 4, 1024
CLOC = 640
NSLOT = NEXP * CLOC
NSLT = NSLOT // 128
XGS = max(NSLOT, 6144)
CAP = 5120.0
EPS = 1e-5
SIM_SINGLE = False


class Sem:
    def __init__(self, stack, nc, name):
        self.h = stack.enter_context(nc.semaphore(name))
        self.val = 0

    def reserve(self, n=1):
        self.val += n
        return self.val


def build_program():
    nc = bass.Bass("TRN2", target_bir_lowering=False)

    x_h = nc.dram_tensor("x", [TOK, D], F32, kind="ExternalInput").ap()
    wqkvT_h = nc.dram_tensor("wqkvT", [128, 3 * 512], F32, kind="ExternalInput").ap()
    wgT_h = nc.dram_tensor("wgT", [128, 64], F32, kind="ExternalInput").ap()
    b1T_h = nc.dram_tensor("b1T", [128, 256], F32, kind="ExternalInput").ap()
    b2_h = nc.dram_tensor("b2", [NEXP, D], F32, kind="ExternalInput").ap()
    w1_h = nc.dram_tensor("w1", [NEXP, NDQ, 128, KCH, DQ], F32, kind="ExternalInput").ap()
    w2_h = nc.dram_tensor("w2", [NEXP, NDQ, 128, 8, D], F32, kind="ExternalInput").ap()
    id128_h = nc.dram_tensor("id128", [128, 128], F32, kind="ExternalInput").ap()
    lstrict_h = nc.dram_tensor("lstrict", [128, 128], F32, kind="ExternalInput").ap()
    allone_h = nc.dram_tensor("allone", [128, 128], F32, kind="ExternalInput").ap()
    rep64_h = nc.dram_tensor("rep64", [64, D], F32, kind="ExternalInput").ap()
    premask_h = nc.dram_tensor("premask", [8, 1], F32, kind="ExternalInput").ap()
    myhot_h = nc.dram_tensor("myhot", [8, 1], F32, kind="ExternalInput").ap()
    y_h = nc.dram_tensor("y", [TOK, D], F32, kind="ExternalOutput").ap()
    cc_in = nc.dram_tensor("cc_in", [8, 8], F32)
    cc_out = nc.dram_tensor("cc_out", [8, 8], F32)
    xg_h = nc.dram_tensor("xg", [NSLOT + 1, D], BF16).ap()
    og_h = nc.dram_tensor("og", [NSLOT + 1, D], F32).ap()

    with ExitStack() as stack:
        sb = lambda n, s, d: stack.enter_context(nc.sbuf_tensor(n, s, d))
        sem = lambda n: Sem(stack, nc, n)

        bigX = sb("bigX", [128, KCH * XGS], BF16)
        w1b = [sb(f"w1b{i}", [128, KCH * DQ], BF16) for i in range(2)]
        w2b = sb("w2b", [128, 8 * D], BF16)
        hbuf = sb("hbuf", [128, max(8 * CLOC, 6144)], BF16)
        etile = [sb(f"etile{i}", [128, 2048], BF16) for i in range(2)]
        stg = [sb(f"stg{i}", [128, D], F32) for i in range(2)]
        gbufs = sb("gbuf", [128, D], F32)
        gbuf2 = nc.alloc_sbuf_tensor_at("gbuf2", [128, D], F32,
                                        offset=nc.lookup_mloc(etile[1]).addr)
        gbuf = [gbufs, gbuf2]
        qk1 = [nc.alloc_sbuf_tensor_at(f"qk1_{i}", [64, 2048], BF16,
                                       offset=nc.lookup_mloc(stg[i]).addr)
               for i in range(2)]
        v1buf = nc.alloc_sbuf_tensor_at("v1buf", [128, D], F32,
                                        offset=nc.lookup_mloc(w1b[1]).addr)
        gbuf3 = nc.alloc_sbuf_tensor_at("gbuf3", [128, D], F32,
                                        offset=nc.lookup_mloc(w2b).addr)
        gbuf4 = nc.alloc_sbuf_tensor_at("gbuf4", [128, D], F32,
                                        offset=nc.lookup_mloc(w2b).addr + 4096)
        ob3 = nc.alloc_sbuf_tensor_at("ob3", [128, D], F32,
                                      offset=nc.lookup_mloc(w1b[0]).addr)
        ob4 = nc.alloc_sbuf_tensor_at("ob4", [128, D], F32,
                                      offset=nc.lookup_mloc(w1b[0]).addr + 4096)
        attnF = sb("attnF", [64, TOK], BF16)
        id128 = sb("id128s", [128, 128], BF16)
        lstrict = sb("lstricts", [128, 128], BF16)
        allone = sb("allones", [128, 128], BF16)
        rep64 = sb("rep64s", [64, D], BF16)
        wqkvT = sb("wqkvTs", [128, 3 * 512], BF16)
        wgT = sb("wgTs", [128, 64], BF16)
        b1T = sb("b1Ts", [128, 16], F32)
        b2b = sb("b2b", [1, D], BF16)
        ones1 = sb("ones1", [1, 128], F32)
        premask = sb("premasks", [8, 1], F32)
        myhot = sb("myhots", [8, 1], F32)
        iota8f = sb("iota8", [128, 12], F32)
        iota8 = iota8f[:, 0:8]
        maskb = sb("maskb", [128, NTILE * 8], BF16)
        g1 = sb("g1", [128, NTILE], F32)
        rankb = sb("rankb", [128, NTILE], F32)
        coefb = sb("coefb", [128, NTILE], F32)
        sloti = sb("sloti", [128, NTILE], I32)
        cntrun = sb("cntrun", [1, 8], F32)
        cntmat = sb("cntmat", [8, 8], F32)
        ccin_s = sb("ccin_s", [8, 8], F32)
        scr = sb("scr", [128, 2 * 1024], F32)
        scrb = sb("scrb", [128, 2048], BF16)

        _hb = nc.lookup_mloc(hbuf).addr
        xbufs = [scr[:, 0:1024], scr[:, 1024:2048],
                 nc.alloc_sbuf_tensor_at("xal0", [128, D], F32, offset=_hb)[:, :],
                 nc.alloc_sbuf_tensor_at("xal1", [128, D], F32, offset=_hb + 4096)[:, :]]
        nbufs = [scrb[:, 0:1024], scrb[:, 1024:2048],
                 nc.alloc_sbuf_tensor_at("nal0", [128, D], BF16, offset=_hb + 8192)[:, :],
                 nc.alloc_sbuf_tensor_at("nal1", [128, D], BF16, offset=_hb + 10240)[:, :]]
        den = sb("den", [128, 16], F32)
        rden = sb("rden", [128, 16], F32)
        ssum = sb("ssum", [128, NTILE], F32)
        ssq = sb("ssq", [128, NTILE], F32)
        negm = sb("negm", [128, NTILE], F32)
        varv = sb("varv", [128, NTILE], F32)
        sdev = varv
        rstd = sb("rstd", [128, NTILE], F32)
        lg = sb("lg", [128, 16], F32)
        gexp = sb("gexp", [128, 16], F32)
        sexp = sb("sexp", [128, NTILE], F32)
        ngmx = sb("ngmx", [128, NTILE], F32)
        gmx = ssum
        tmp8 = sb("tmp8", [128, 8], F32)
        tmp8b = sb("tmp8b", [128, 8], F32)
        tmp1 = den[:, 0:4]
        epsb = iota8f[:, 8:9]
        posb = sexp
        keepb = gmx

        s_bar = sem("bar")
        s_eps = sem("eps")
        s_c = sem("cload")
        s_ci = sem("ciota")
        s_z = sem("zfill")
        s_zg = sem("zfillg")
        s_xin = [sem("xin0"), sem("xin1"), sem("xin2"), sem("xin3")]
        s_xfree = sem("xfree")
        s_vst = sem("vst")
        s_sqr = sem("sqr")
        s_xn = sem("xn")
        s_tr = sem("tr")
        s_trv = {"s": sem("trvs"), "v": sem("trvv"), "g": sem("trvg")}
        s_gst = sem("gst")
        s_mmg = sem("mmg")
        s_vg = sem("vg")
        s_sg = sem("sg")
        s_mmp = sem("mmp")
        s_idx = sem("idx")
        s_sct = [sem("sct0"), sem("sct1"), sem("sct2"), sem("sct3")]
        s_cc = sem("cc")
        s_ccd = sem("ccd")
        s_mma = sem("mma")
        s_va = sem("va")
        s_sa = sem("sa")
        s_gl = [sem("gl0"), sem("gl1")]
        s_tr2 = sem("tr2")
        s_trv2 = sem("trv2")
        s_w = sem("w")
        s_mmh = sem("mmh")
        s_relu = sem("relu")
        s_mmo = sem("mmo")
        s_oev = sem("oev")
        s_oac = [sem("oac0"), sem("oac1")]
        s_oacy = [sem("oacy0"), sem("oacy1")]
        s_og = [sem("og0"), sem("og1"), sem("og2"), sem("og3")]
        s_rep = sem("rep")
        s_out = sem("out")
        s_st = [sem("st0"), sem("st1"), sem("st2"), sem("st3")]

        TX, VX, SX, GX, YX = [], [], [], [], []

        _fsems = {k: stack.enter_context(nc.semaphore(f"flow_{k}"))
                  for k in ("t", "v", "s", "g", "y")}
        _fcnt = {k: 0 for k in _fsems}

        def flow(eng, key):
            _fcnt[key] += 1
            eng.drain().then_inc(_fsems[key], 1)
            eng.wait_ge(_fsems[key], _fcnt[key])

        def edge(eng, key, inst):
            _fcnt[key] += 1
            inst.then_inc(_fsems[key], 1)
            eng.wait_ge(_fsems[key], _fcnt[key])

        def cover(eng, key):
            eng.wait_ge(_fsems[key], _fcnt[key])

        def v_eps(v):
            v.memset(epsb, EPS)
            v.memset(etile[0][:, :], 0.0)
            v.memset(cntrun[:, :], 0.0)
            v.drain().then_inc(s_eps.h, 1)
        VX.append(v_eps)
        s_eps.reserve(1)

        STATS_ON_POOL = False
        AXC = mybir.AxisListType.C
        nbar = [0]

        def barrier():
            nbar[0] += 1
            n = nbar[0]
            for L in (TX, VX, SX, GX, YX):
                def f(e, n=n):
                    e.drain().then_inc(s_bar.h, 1)
                    e.wait_ge(s_bar.h, 5 * n)
                L.append(f)

        def g_consts(g):
            g.dma_start(out=wqkvT[:, :], in_=wqkvT_h).then_inc(s_c.h, 16)
            g.dma_start(out=wgT[:, :], in_=wgT_h).then_inc(s_c.h, 16)
            g.dma_start(out=b1T[:, 0:8], in_=b1T_h[:, 0:8]).then_inc(s_c.h, 16)
            g.dma_start(out=id128[:, :], in_=id128_h).then_inc(s_c.h, 16)
            g.dma_start(out=lstrict[:, :], in_=lstrict_h).then_inc(s_c.h, 16)
            g.dma_start(out=allone[:, :], in_=allone_h).then_inc(s_c.h, 16)
            g.dma_start(out=rep64[:, :], in_=rep64_h).then_inc(s_c.h, 16)
            g.dma_start(out=ones1[:, :], in_=allone_h[0:1, :]).then_inc(s_c.h, 16)
            g.dma_start(out=premask[:, :], in_=premask_h).then_inc(s_c.h, 16)
            g.dma_start(out=myhot[:, :], in_=myhot_h).then_inc(s_c.h, 16)
            g.iota(iota8, pattern=[[1, 8]], base=0, channel_multiplier=0,
                   allow_small_or_imprecise_dtypes=True).then_inc(s_ci.h, 1)
            g.dma_start(out=b2b[:, :], in_=b2_h[0:1, :]).then_inc(s_w.h, 16)
            g.dma_start(out=w1b[0][:, :], in_=w1_h[0, 0]).then_inc(s_w.h, 16)
            g.dma_start(out=w2b[:, :], in_=w2_h[0, 0]).then_inc(s_w.h, 16)
            g.wait_ge(s_eps.h, 1)
            g.dma_start(out=og_h[NSLOT:NSLOT + 1, :], in_=etile[0][0:1, 0:D]).then_inc(s_zg.h, 16)
        GX.append(g_consts)
        s_zg.reserve(16)

        ZBLK = (NSLOT + 1 + 255) // 256

        def y_zfill(y):
            y.wait_ge(s_eps.h, 1)
            for blk in range(ZBLK):
                r0 = blk * 256
                r1 = min(r0 + 256, NSLOT + 1)
                n_el = (r1 - r0) * D
                zsrc = etile[0][:, :] if n_el == 128 * 2048 else etile[0][0:1, 0:n_el]
                y.dma_start(out=xg_h[r0:r1, :], in_=zsrc).then_inc(s_z.h, 16)
        YX.append(y_zfill)
        s_c.reserve(10 * 16)
        s_ci.reserve(1)
        s_w.reserve(3 * 16)
        zfv = s_z.reserve(ZBLK * 16)

        ps1 = ExitStack()
        PTR_SLOTS = 3
        ptr_sl = [ps1.enter_context(nc.psum_tensor(f"ptr{i}", [128, 128], BF16))[:, :]
                  for i in range(PTR_SLOTS)]
        pcnt0 = ps1.enter_context(nc.psum_tensor("pcnt0", [1, 8], F32))
        plog = [ps1.enter_context(nc.psum_tensor(f"plog{i}", [128, 8], F32)) for i in range(2)]
        ppref = [ps1.enter_context(nc.psum_tensor(f"ppref{i}", [128, 8], F32)) for i in range(2)]
        vg_done_log = [None, None]
        vg_done_pref = [0, 0]

        def y_xload(t, slot, v_need):
            def f(y, t=t, slot=slot, v_need=v_need):
                if v_need:
                    y.wait_ge(s_xn.h, v_need)
                y.dma_start(out=xbufs[slot],
                            in_=x_h[t * 128:(t + 1) * 128, :]).then_inc(s_xin[slot].h, 16)
            YX.append(f)
            return s_xin[slot].reserve(16)

        NS = 4
        xin_val = [0] * NTILE
        for _t in range(NS):
            xin_val[_t] = y_xload(_t, _t, 0)
        sct_of_tile = [0] * (NTILE + NS)
        TS = [dict() for _ in range(NTILE)]

        def st_AB(t):
            slot = t % NS
            xiv = xin_val[t]

            def c_stat(s, t=t, slot=slot, xiv=xiv):
                xt = xbufs[slot]
                s.wait_ge(s_xin[slot].h, xiv)
                s.activation(stg[0][:, :], xt, AF.Copy,
                             accum_out=ssum[:, t:t + 1]).then_inc(s_vst.h, 1)
                s.activation(stg[1][:, :], xt, AF.Square,
                             accum_out=ssq[:, t:t + 1]).then_inc(s_vst.h, 1)
            SX.append(c_stat)
            vstv = s_vst.reserve(2)

            if STATS_ON_POOL:
                def g_stats(g, t=t, vstv=vstv):
                    g.wait_ge(s_vst.h, vstv)
                    edge(g, "g", g.tensor_scalar_mul(negm[:, t:t + 1], ssum[:, t:t + 1],
                                                     -1.0 / D))
                    edge(g, "g", g.tensor_mul(rstd[:, t:t + 1], negm[:, t:t + 1],
                                              negm[:, t:t + 1]))
                    edge(g, "g", g.tensor_scalar_mul(varv[:, t:t + 1], ssq[:, t:t + 1],
                                                     1.0 / D))
                    g.tensor_sub(varv[:, t:t + 1], varv[:, t:t + 1],
                                 rstd[:, t:t + 1]).then_inc(s_gst.h, 1)
                GX.append(g_stats)
            else:
                def v_stats(v, t=t, vstv=vstv):
                    v.wait_ge(s_vst.h, vstv)
                    edge(v, "v", v.tensor_scalar(negm[:, t:t + 1], ssum[:, t:t + 1],
                                                 -1.0 / D, None, OP.mult))
                    edge(v, "v", v.tensor_mul(rstd[:, t:t + 1], negm[:, t:t + 1],
                                              negm[:, t:t + 1]))
                    edge(v, "v", v.tensor_scalar(varv[:, t:t + 1], ssq[:, t:t + 1],
                                                 1.0 / D, None, OP.mult))
                    v.tensor_sub(varv[:, t:t + 1], varv[:, t:t + 1],
                                 rstd[:, t:t + 1]).then_inc(s_gst.h, 1)
                VX.append(v_stats)
            TS[t]["vvv2"] = s_gst.reserve()

        def st_CD(t):
            slot = t % NS
            xiv = xin_val[t]
            sct_prev = sct_of_tile[t]
            vvv2 = TS[t]["vvv2"]

            def c_rstd(s, t=t, vvv2=vvv2):
                s.wait_ge(s_eps.h, 1)
                s.wait_ge(s_gst.h, vvv2)
                edge(s, "s", s.activation(sdev[:, t:t + 1], varv[:, t:t + 1], AF.Ln,
                                          bias=epsb, scale=1.0))
                s.activation(rstd[:, t:t + 1], sdev[:, t:t + 1], AF.Exp,
                             bias=0.0, scale=-0.5).then_inc(s_sqr.h, 1)
            SX.append(c_rstd)
            sqv = s_sqr.reserve()

            def v_xn(v, t=t, slot=slot, sqv=sqv, xiv=xiv, sct_prev=sct_prev):
                xt = xbufs[slot]
                xb = nbufs[slot]
                v.wait_ge(s_sqr.h, sqv)
                v.wait_ge(s_xin[slot].h, xiv)
                if t >= NS:
                    v.wait_ge(s_tr.h, 8 * (t - NS + 1))
                    v.wait_ge(s_sct[t % 4].h, sct_prev)
                v.tensor_scalar(xb, xt, negm[:, t:t + 1], rstd[:, t:t + 1],
                                OP.add, OP.mult).then_inc(s_xn.h, 1)
            VX.append(v_xn)
            TS[t]["xnv"] = s_xn.reserve()
            if t + NS < NTILE:
                xin_val[t + NS] = y_xload(t + NS, (t + NS) % NS, TS[t]["xnv"])

        EV_ENG = ["s", "v", "s", "v", "s", "v", "s", "v"]

        def st_EF(t):
            slot = t % NS
            xnv = TS[t]["xnv"]
            TS[t]["ev"] = {}
            for k in range(KCH):
                psl = k % PTR_SLOTS
                ek = EV_ENG[k]
                kb = k - PTR_SLOTS
                if kb >= 0:
                    tdv = TS[t]["ev"][kb]
                elif t > 0:
                    tdv = TS[t - 1]["ev"][kb + KCH]
                else:
                    tdv = None

                def t_tr(e, t=t, k=k, slot=slot, psl=psl, xnv=xnv, tdv=tdv):
                    e.wait_ge(s_c.h, 160)
                    e.wait_ge(s_xn.h, xnv)
                    if tdv:
                        e.wait_ge(s_trv[tdv[0]].h, tdv[1])
                    e.transpose(ptr_sl[psl],
                                nbufs[slot][:, k * 128:(k + 1) * 128],
                                id128[:, :]).then_inc(s_tr.h, 1)
                TX.append(t_tr)
                trv = s_tr.reserve()

                dst_sl = slice(k * TOK + t * 128, k * TOK + t * 128 + 128)
                if ek == "s":
                    def c_trev(s, psl=psl, trv=trv, dst_sl=dst_sl):
                        s.wait_ge(s_tr.h, trv)
                        s.copy(bigX[:, dst_sl], ptr_sl[psl]).then_inc(s_trv["s"].h, 1)
                    SX.append(c_trev)
                elif ek == "v":
                    def v_trev(v, psl=psl, trv=trv, dst_sl=dst_sl):
                        v.wait_ge(s_tr.h, trv)
                        v.tensor_copy(bigX[:, dst_sl], ptr_sl[psl]).then_inc(s_trv["v"].h, 1)
                    VX.append(v_trev)
                else:
                    def g_trev(g, psl=psl, trv=trv, dst_sl=dst_sl):
                        g.wait_ge(s_tr.h, trv)
                        g.tensor_copy(bigX[:, dst_sl], ptr_sl[psl]).then_inc(s_trv["g"].h, 1)
                    GX.append(g_trev)
                TS[t]["ev"][k] = (ek, s_trv[ek].reserve())

        def st_GHIJ(t):
            bank = t % 2
            vdl = vg_done_log[bank]
            trv_now = {ek: s_trv[ek].val for ek in ("s", "v", "g")}

            def t_logit(e, t=t, bank=bank, vdl=vdl, trv_now=dict(trv_now)):
                for ek, tv in trv_now.items():
                    if tv:
                        e.wait_ge(s_trv[ek].h, tv)
                if vdl:
                    e.wait_ge(s_vg.h, vdl[0])
                    e.wait_ge(s_sg.h, vdl[1])
                for k in range(KCH):
                    inst = e.matmul(plog[bank][:, :],
                                    bigX[:, k * TOK + t * 128: k * TOK + t * 128 + 128],
                                    wgT[:, k * 8:(k + 1) * 8],
                                    start=(k == 0), stop=(k == KCH - 1))
                inst.then_inc(s_mmg.h, 1)
            TX.append(t_logit)
            mgv = s_mmg.reserve()
            sg_prev = s_sg.val

            def v_gate(v, t=t, bank=bank, mgv=mgv):
                v.wait_ge(s_mmg.h, mgv)
                cover(v, "v")
                v.reduce_max(ngmx[:, t:t + 1], plog[bank][:, :], axis=AX, negate=True)
                edge(v, "v", v.reduce_max(gmx[:, t:t + 1], plog[bank][:, :], axis=AX))
                edge(v, "v", v.tensor_scalar(tmp8[:, :], plog[bank][:, :],
                                             gmx[:, t:t + 1], None, OP.is_equal))
                edge(v, "v", v.tensor_tensor_scan(tmp8b[:, :], tmp8[:, :], tmp8[:, :],
                                                  0.0, OP.add, OP.bypass))
                edge(v, "v", v.tensor_scalar(tmp8b[:, :], tmp8b[:, :], 1.0, None,
                                             OP.is_equal))
                v.tensor_mul(maskb[:, t * 8:(t + 1) * 8], tmp8[:, :], tmp8b[:, :]
                             ).then_inc(s_vg.h, 1)
            VX.append(v_gate)
            vgm = s_vg.reserve()

            def c_gexp(s, t=t, bank=bank, vgm=vgm):
                s.wait_ge(s_vg.h, vgm)
                s.activation(gexp[:, bank * 8:(bank + 1) * 8], plog[bank][:, :],
                             AF.Exp, bias=ngmx[:, t:t + 1], scale=1.0,
                             accum_out=sexp[:, t:t + 1]).then_inc(s_sg.h, 1)
            SX.append(c_gexp)
            sgv = s_sg.reserve()
            vg_done_log[bank] = (vgm, sgv)

            vdp = vg_done_pref[bank]
            prefv = TS[t - 1]["prefv"] if t > 0 else 0

            def t_pref(e, t=t, bank=bank, vgm=vgm, vdp=vdp, prefv=prefv):
                e.wait_ge(s_c.h, 160)
                e.wait_ge(s_eps.h, 1)
                e.wait_ge(s_vg.h, vgm)
                if prefv:
                    e.wait_ge(s_vg.h, prefv)
                if vdp:
                    e.wait_ge(s_vg.h, vdp)
                e.matmul(ppref[bank][:, :], lstrict[:, :], maskb[:, t * 8:(t + 1) * 8],
                         start=True, stop=False)
                e.matmul(ppref[bank][:, :], ones1[:, :], cntrun[:, :],
                         start=False, stop=True)
                e.matmul(pcnt0[:, :], allone[:, 0:1], maskb[:, t * 8:(t + 1) * 8],
                         start=True, stop=True).then_inc(s_mmp.h, 1)
            TX.append(t_pref)
            TS[t]["mpv"] = s_mmp.reserve()
            TS[t]["sgv"] = sgv

        def st_K(t):
            bank = t % 2
            slot = t % NS
            mpv, sgv = TS[t]["mpv"], TS[t]["sgv"]

            mulv = s_vg.val + 1

            def v_pref(v, t=t, bank=bank, mpv=mpv, sgv=sgv, mulv=mulv):
                v.wait_ge(s_mmp.h, mpv)
                v.wait_ge(s_sg.h, sgv)
                cover(v, "v")
                v.reciprocal(g1[:, t:t + 1], sexp[:, t:t + 1])
                v.tensor_mul(tmp8[:, :], maskb[:, t * 8:(t + 1) * 8],
                             ppref[bank][:, :]).then_inc(s_vg.h, 1)
                v.wait_ge(s_vg.h, mulv)
                edge(v, "v", v.reduce_sum(rankb[:, t:t + 1], tmp8[:, :], axis=AX))
                v.tensor_add(cntrun[:, :], cntrun[:, :],
                             pcnt0[:, :]).then_inc(s_vg.h, 1)
            VX.append(v_pref)
            s_vg.reserve()
            vg_done_pref[bank] = s_vg.val
            TS[t]["prefv"] = s_vg.reserve()

            def v_idx(v, t=t, bank=bank):
                v.wait_ge(s_ci.h, 1)
                cover(v, "v")
                mk = maskb[:, t * 8:(t + 1) * 8]
                edge(v, "v", v.scalar_tensor_tensor(tmp8b[:, :], mk, 1.0, iota8,
                                                    OP.mult, OP.mult,
                                                    accum_out=tmp1[:, 1:2]))
                v.tensor_scalar(tmp1[:, 3:4], rankb[:, t:t + 1], float(CLOC), None, OP.is_lt)
                edge(v, "v", v.tensor_scalar(tmp1[:, 2:3], tmp1[:, 1:2], float(CLOC),
                                             rankb[:, t:t + 1], OP.mult, OP.add))
                edge(v, "v", v.tensor_scalar(tmp1[:, 2:3], tmp1[:, 2:3], -float(NSLOT),
                                             None, OP.add))
                edge(v, "v", v.tensor_scalar(tmp1[:, 2:3], tmp1[:, 2:3], tmp1[:, 3:4],
                                             float(NSLOT), OP.mult, OP.add))
                v.tensor_copy(sloti[:, t:t + 1], tmp1[:, 2:3]).then_inc(s_idx.h, 1)
            VX.append(v_idx)
            s_idx.reserve()
            idxv = s_idx.val

            def g_scat(g, t=t, slot=slot, idxv=idxv, zfv=zfv):
                g.wait_ge(s_z.h, zfv)
                g.wait_ge(s_idx.h, idxv)
                g.indirect_dma_start(
                    out=xg_h, out_offset=IndirectOffsetOnAxis(sloti[:, t:t + 1], 0),
                    in_=nbufs[slot], in_offset=None,
                ).then_inc(s_sct[t % 4].h, 16)
            GX.append(g_scat)
            sct_of_tile[t + NS] = s_sct[t % 4].reserve(16)

        for p in range(NTILE + 4):
            if p >= 4:
                st_K(p - 4)
            if 3 <= p < NTILE + 3:
                st_GHIJ(p - 3)
            if 2 <= p < NTILE + 2:
                st_EF(p - 2)
            if 1 <= p < NTILE + 1:
                st_CD(p - 1)
            if p < NTILE:
                st_AB(p)

        pbc8 = ppref[0][0:8, 0:8]
        vgv_all = s_vg.val
        idx_all = s_idx.val
        def t_bc8(e, vgv_all=vgv_all, idx_all=idx_all):
            e.wait_ge(s_c.h, 160)
            e.wait_ge(s_vg.h, vgv_all)
            e.wait_ge(s_idx.h, idx_all)
            e.matmul(pbc8, ones1[:, 0:8], cntrun[:, :], start=True,
                     stop=True).then_inc(s_mmp.h, 1)
        TX.append(t_bc8)
        mpv = s_mmp.reserve()

        def v_ccin(v, mpv=mpv):
            v.wait_ge(s_mmp.h, mpv)
            v.tensor_scalar(ccin_s[:, :], pbc8, myhot[:, :], None,
                            OP.mult).then_inc(s_vg.h, 1)
        VX.append(v_ccin)
        ccin_ready = s_vg.reserve()
        ccv = s_c.reserve(16)
        s_cc.reserve(1)
        s_ccd.reserve(16)

        def g_cc(g, ccin_ready=ccin_ready, ccv=ccv):
            g.wait_ge(s_vg.h, ccin_ready)
            g.dma_start(out=cc_in.ap(), in_=ccin_s[:, :]).then_inc(s_c.h, 16)
            g.wait_ge(s_c.h, ccv)
            g.collective_compute(
                "AllReduce", OP.add,
                replica_groups=[[0]] if SIM_SINGLE else [list(range(N_CORES))],
                ins=[cc_in.ap()],
                outs=[cc_out.ap()],
            ).then_inc(s_cc.h, 1)
            g.wait_ge(s_cc.h, 1)
            g.dma_start(out=cntmat[:, :], in_=cc_out.ap()).then_inc(s_ccd.h, 16)
        sctv = [s_sct[i].val for i in range(4)]

        def g_sctwait(g, sctv=tuple(sctv)):
            for i in range(4):
                g.wait_ge(s_sct[i].h, sctv[i])
        GX.append(g_sctwait)

        ps1.close()
        barrier()
        GX.append(g_cc)

        def y_xgt67(y):
            for k in (6, 7):
                y.dma_start(out=bigX[:, k * XGS: k * XGS + NSLOT],
                            in_=xg_h[0:NSLOT, k * 128:(k + 1) * 128],
                            transpose=True).then_inc(s_tr2.h, 16)
        YX.append(y_xgt67)
        s_tr2.reserve(2 * 16)

        va_qk = [0, 0]
        va_v = [0, 0]
        sa_done = [0]
        qks = [(hbuf[0:64, 0:2048], hbuf[0:64, 2048:4096]), (qk1[0][:, :], qk1[1][:, :])]
        vbufs = [gbufs, v1buf]
        with ExitStack() as psA:
            pqk = [psA.enter_context(nc.psum_tensor(f"pqk{i}", [64, 512], F32)) for i in range(2)]
            pv = [psA.enter_context(nc.psum_tensor(f"pv{i}", [128, 64], F32)) for i in range(2)]
            for b in range(2):
                base = b * 2048
                for qk in range(2):
                    for i4 in range(4):
                        bank = (qk * 4 + i4) % 2
                        vqv = va_qk[bank]

                        def t_qk(e, qk=qk, i4=i4, bank=bank, base=base, vqv=vqv):
                            e.wait_ge(s_c.h, 160)
                            if vqv:
                                e.wait_ge(s_va.h, vqv)
                            for k in range(KCH):
                                inst = e.matmul(pqk[bank][:, :],
                                                wqkvT[:, qk * 512 + k * 64: qk * 512 + (k + 1) * 64],
                                                bigX[:, k * TOK + base + i4 * 512: k * TOK + base + i4 * 512 + 512],
                                                start=(k == 0), stop=(k == KCH - 1))
                            inst.then_inc(s_mma.h, 1)
                        TX.append(t_qk)
                        mav = s_mma.reserve()

                        def v_qk(v, b=b, qk=qk, i4=i4, bank=bank, mav=mav):
                            v.wait_ge(s_mma.h, mav)
                            dst = qks[b][qk][:, i4 * 512:(i4 + 1) * 512]
                            v.tensor_copy(dst, pqk[bank][:, :]).then_inc(s_va.h, 1)
                        VX.append(v_qk)
                        va_qk[bank] = s_va.reserve()
                for it in range(16):
                    bank = it % 2
                    vvv = va_v[bank]

                    def t_v(e, it=it, bank=bank, base=base, vvv=vvv):
                        e.wait_ge(s_c.h, 160)
                        if vvv:
                            e.wait_ge(s_va.h, vvv)
                        for k in range(KCH):
                            inst = e.matmul(pv[bank][:, :],
                                            bigX[:, k * TOK + base + it * 128: k * TOK + base + it * 128 + 128],
                                            wqkvT[:, 1024 + k * 64: 1024 + (k + 1) * 64],
                                            start=(k == 0), stop=(k == KCH - 1))
                        inst.then_inc(s_mma.h, 1)
                    TX.append(t_v)
                    mav = s_mma.reserve()

                    def v_v(v, b=b, it=it, bank=bank, mav=mav):
                        v.wait_ge(s_mma.h, mav)
                        v.tensor_copy(vbufs[b][:, it * 64:(it + 1) * 64],
                                      pv[bank][:, :]).then_inc(s_va.h, 1)
                    VX.append(v_v)
                    va_v[bank] = s_va.reserve()

        qkv_done = s_mma.val

        def y_xgt05(y, qkv_done=qkv_done):
            y.wait_ge(s_mma.h, qkv_done)
            for k in range(6):
                y.dma_start(out=bigX[:, k * XGS: k * XGS + NSLOT],
                            in_=xg_h[0:NSLOT, k * 128:(k + 1) * 128],
                            transpose=True).then_inc(s_tr2.h, 16)
        YX.append(y_xgt05)
        s_tr2.reserve(6 * 16)

        for b in range(2):
            base = b * 2048
            with ExitStack() as psB:
                pscore = psB.enter_context(nc.psum_tensor(f"pscore{b}", [128, 2048], F32))
                pao = [psB.enter_context(nc.psum_tensor(f"pao{b}{i}", [64, 512], F32)) for i in range(4)]
                qk_done = s_va.val
                vpcol = 4096 + b * 1024
                for kt in range(16):
                    ebank = kt % 2
                    sdv = sa_done[0]
                    mma_prev = s_mma.val

                    def t_sc(e, b=b, kt=kt, sdv=sdv, qk_done=qk_done, pscore=pscore):
                        e.wait_ge(s_va.h, qk_done)
                        if sdv:
                            e.wait_ge(s_sa.h, sdv)
                        for i4 in range(4):
                            e.matmul(pscore[:, i4 * 512:(i4 + 1) * 512],
                                     qks[b][1][:, kt * 128:(kt + 1) * 128],
                                     qks[b][0][:, i4 * 512:(i4 + 1) * 512],
                                     start=True, stop=True).then_inc(s_mma.h, 1)
                    TX.append(t_sc)
                    mavs = [s_mma.reserve() for _ in range(4)]

                    def v_max(v, kt=kt, mavs=tuple(mavs), pscore=pscore):
                        for i4 in range(4):
                            v.wait_ge(s_mma.h, mavs[i4])
                            v.reduce_max(tmp1[:, i4:i4 + 1],
                                         pscore[:, i4 * 512:(i4 + 1) * 512], axis=AX)
                        flow(v, "v")
                        v.reduce_max(ngmx[:, kt:kt + 1], tmp1[:, 0:4], axis=AX,
                                     negate=True).then_inc(s_va.h, 1)
                    VX.append(v_max)
                    vmx = s_va.reserve()

                    def c_exp(s, kt=kt, ebank=ebank, vmx=vmx, mma_prev=mma_prev, pscore=pscore,
                              zfv=zfv):
                        s.wait_ge(s_z.h, zfv)
                        s.wait_ge(s_zg.h, 16)
                        s.wait_ge(s_va.h, vmx)
                        if kt >= 2:
                            s.wait_ge(s_mma.h, mma_prev)
                        s.activation(etile[ebank][:, :], pscore[:, :],
                                     AF.Exp, bias=ngmx[:, kt:kt + 1], scale=1.0,
                                     accum_out=den[:, kt:kt + 1]).then_inc(s_sa.h, 1)
                    SX.append(c_exp)
                    sa_done[0] = s_sa.reserve()
                    sav = s_sa.val

                    def v_vp(v, b=b, kt=kt, sav=sav, vpcol=vpcol):
                        v.wait_ge(s_sa.h, sav)
                        flow(v, "v")
                        edge(v, "v", v.reciprocal(rden[:, kt:kt + 1], den[:, kt:kt + 1]))
                        v.tensor_scalar(hbuf[:, vpcol + kt * 64: vpcol + (kt + 1) * 64],
                                        vbufs[b][:, kt * 64:(kt + 1) * 64],
                                        rden[:, kt:kt + 1], None, OP.mult
                                        ).then_inc(s_va.h, 1)
                    VX.append(v_vp)
                    vpv = s_va.reserve()

                    for i4 in range(4):
                        def t_ao(e, b=b, kt=kt, i4=i4, ebank=ebank, vpv=vpv, pao=pao,
                                 vpcol=vpcol):
                            e.wait_ge(s_va.h, vpv)
                            e.matmul(pao[i4][:, :],
                                     hbuf[:, vpcol + kt * 64: vpcol + (kt + 1) * 64],
                                     etile[ebank][:, i4 * 512:(i4 + 1) * 512],
                                     start=(kt == 0), stop=(kt == 15)
                                     ).then_inc(s_mma.h, 1)
                        TX.append(t_ao)
                        s_mma.reserve()
                mma_fin = s_mma.val

                def v_ao(v, base=base, mma_fin=mma_fin, pao=pao):
                    v.wait_ge(s_mma.h, mma_fin)
                    for i4 in range(4):
                        v.tensor_copy(attnF[:, base + i4 * 512: base + i4 * 512 + 512],
                                      pao[i4][:, :])
                    v.drain().then_inc(s_va.h, 1)
                VX.append(v_ao)
                s_va.reserve()

        barrier()

        def record_p2b(ps_scope):
            poff = ps_scope.enter_context(nc.psum_tensor("poff", [1, 8], F32))
            pobc2 = ps_scope.enter_context(nc.psum_tensor("pobc2", [128, 8], F32))
            ccdv = s_ccd.val

            def t_off(e, ccdv=ccdv):
                e.wait_ge(s_c.h, 160)
                e.wait_ge(s_ccd.h, ccdv)
                e.matmul(poff[:, :], premask[:, :], cntmat[:, :], start=True,
                         stop=True).then_inc(s_mmp.h, 1)
            TX.append(t_off)
            mpv = s_mmp.reserve()

            def v_poke(v, mpv=mpv):
                v.wait_ge(s_mmp.h, mpv)
                flow(v, "v")
                v.tensor_copy(tmp8[0:1, :], poff[:, :]).then_inc(s_vg.h, 1)
            VX.append(v_poke)
            offrow_ready = s_vg.reserve()

            def t_obc2(e, offrow_ready=offrow_ready):
                e.wait_ge(s_vg.h, offrow_ready)
                e.matmul(pobc2[:, :], ones1[:, :], tmp8[0:1, :], start=True,
                         stop=True).then_inc(s_mmp.h, 1)
            TX.append(t_obc2)
            mpv = s_mmp.reserve()

            oaw = (s_oac[0].val, s_oac[1].val, s_oacy[0].val, s_oacy[1].val)

            def v_keep(v, mpv=mpv, oaw=oaw):
                v.wait_ge(s_mmp.h, mpv)
                v.wait_ge(s_oac[0].h, oaw[0])
                v.wait_ge(s_oac[1].h, oaw[1])
                v.wait_ge(s_oacy[0].h, oaw[2])
                v.wait_ge(s_oacy[1].h, oaw[3])
                flow(v, "v")
                for t in range(NTILE):
                    v.scalar_tensor_tensor(stg[0][:, t * 8:(t + 1) * 8],
                                           maskb[:, t * 8:(t + 1) * 8], 1.0,
                                           pobc2[:, :], OP.mult, OP.mult,
                                           accum_out=posb[:, t:t + 1])
                flow(v, "v")
                v.tensor_add(posb[:, :], posb[:, :], rankb[:, :])
                flow(v, "v")
                v.tensor_scalar(keepb[:, :], posb[:, :], CAP, None, OP.is_lt)
                flow(v, "v")
                v.tensor_mul(coefb[:, :], keepb[:, :], g1[:, :])
                v.drain().then_inc(s_vg.h, 1)
            VX.append(v_keep)
            return s_vg.reserve()

        xgt_a = s_tr2.val

        def t_xgtwait(e, xgt_a=xgt_a):
            e.wait_ge(s_tr2.h, xgt_a)
        TX.append(t_xgtwait)
        barrier()

        ps4 = ExitStack()
        ph = [ps4.enter_context(nc.psum_tensor(f"ph{i}", [128, 512], F32)) for i in range(2)]
        po = [ps4.enter_context(nc.psum_tensor(f"po{i}", [128, 512], F32)) for i in range(4)]
        relu_done = [0, 0]
        oev_done = [0, 0, 0, 0]
        stg_last = [None, None]
        oac_region = {}
        CHUNKS = [(0, 512), (512, 128)]

        wv1_next = [s_w.val]
        for pi in range(NEXP * NDQ):
            e_, dq = divmod(pi, NDQ)
            wv1 = wv1_next[0]
            wv = s_w.val
            wbuf = w1b[pi % 2]
            mo_prev = s_mmo.val
            for (coff, cw) in CHUNKS:
                for fd in range(8):
                    bank = fd % 2
                    rdv = relu_done[bank]

                    def t_h(e, e_=e_, coff=coff, cw=cw, fd=fd, bank=bank, wv1=wv1,
                            rdv=rdv, wbuf=wbuf):
                        e.wait_ge(s_w.h, wv1)
                        if rdv:
                            e.wait_ge(s_relu.h, rdv)
                        for k in range(KCH):
                            inst = e.matmul(
                                ph[bank][:, 0:cw],
                                wbuf[:, k * DQ + fd * 128: k * DQ + fd * 128 + 128],
                                bigX[:, k * XGS + e_ * CLOC + coff: k * XGS + e_ * CLOC + coff + cw],
                                start=(k == 0), stop=(k == KCH - 1))
                        inst.then_inc(s_mmh.h, 1)
                    TX.append(t_h)
                    mhv = s_mmh.reserve()

                    def c_relu(s, e_=e_, dq=dq, coff=coff, cw=cw, fd=fd, bank=bank,
                               mhv=mhv, mo_prev=mo_prev, pi=pi):
                        if mo_prev:
                            s.wait_ge(s_mmo.h, mo_prev)
                        s.wait_ge(s_mmh.h, mhv)
                        s.activation(hbuf[:, fd * CLOC + coff: fd * CLOC + coff + cw],
                                     ph[bank][:, 0:cw], AF.Relu,
                                     bias=b1T[:, (pi % 2) * 8 + fd: (pi % 2) * 8 + fd + 1],
                                     scale=1.0).then_inc(s_relu.h, 1)
                    SX.append(c_relu)
                    relu_done[bank] = s_relu.reserve()
            if pi + 1 < NEXP * NDQ:
                ne, ndq = divmod(pi + 1, NDQ)
                mh_cur = s_mmh.val
                wv_prev = s_w.val
                nbuf = w1b[(pi + 1) % 2]
                mh_free = mh_cur - 16

                def g_pref1(g, ne=ne, ndq=ndq, mh_free=mh_free, nbuf=nbuf, pi=pi,
                            wv_prev=wv_prev):
                    g.wait_ge(s_w.h, wv_prev)
                    if mh_free > 0:
                        g.wait_ge(s_mmh.h, mh_free)
                    g.dma_start(out=b1T[:, ((pi + 1) % 2) * 8: ((pi + 1) % 2) * 8 + 8],
                                in_=b1T_h[:, ne * 32 + ndq * 8: ne * 32 + ndq * 8 + 8]
                                ).then_inc(s_w.h, 16)
                    g.dma_start(out=nbuf[:, :], in_=w1_h[ne, ndq]).then_inc(s_w.h, 16)
                GX.append(g_pref1)
                s_w.reserve(32)
                wv1_next[0] = s_w.val
            rall = s_relu.val
            for st_ in range(CLOC // 128):
                stgbuf = stg[st_ % 2]
                oacv = stg_last[st_ % 2]
                for nf in range(2):
                    bank = (st_ * 2 + nf) % 4
                    odv = oev_done[bank]

                    def t_o(e, dq=dq, st_=st_, nf=nf, bank=bank, rall=rall, odv=odv, wv=wv):
                        e.wait_ge(s_w.h, wv)
                        e.wait_ge(s_relu.h, rall)
                        if odv:
                            e.wait_ge(s_oev.h, odv)
                        for fd in range(8):
                            last = (fd == 7) and (dq != 0)
                            inst = e.matmul(po[bank][:, :],
                                            hbuf[:, fd * CLOC + st_ * 128: fd * CLOC + st_ * 128 + 128],
                                            w2b[:, fd * D + nf * 512: fd * D + nf * 512 + 512],
                                            start=(fd == 0), stop=last)
                        if dq == 0:
                            inst = e.matmul(po[bank][:, :], allone[0:1, :],
                                            b2b[:, nf * 512:(nf + 1) * 512],
                                            start=False, stop=True)
                        inst.then_inc(s_mmo.h, 1)
                    TX.append(t_o)
                    mov = s_mmo.reserve()

                    def v_oev(v, st_=st_, nf=nf, bank=bank, mov=mov, oacv=oacv, stgbuf=stgbuf):
                        v.wait_ge(s_mmo.h, mov)
                        if oacv is not None:
                            v.wait_ge(oacv[0].h, oacv[1])
                        v.tensor_copy(stgbuf[:, nf * 512:(nf + 1) * 512],
                                      po[bank][:, :]).then_inc(s_oev.h, 1)
                    VX.append(v_oev)
                    oev_done[bank] = s_oev.reserve()
                oev_now = s_oev.val
                row0 = e_ * CLOC + st_ * 128
                reg_prev = oac_region.get(st_) if dq != 0 else None
                gw_prev = s_oac[st_ % 2].val

                if dq == 0:
                    def y_oac(y, row0=row0, oev_now=oev_now, stgbuf=stgbuf, st_=st_):
                        y.wait_ge(s_oev.h, oev_now)
                        y.dma_start(out=og_h[row0:row0 + 128, :], in_=stgbuf[:, :]
                                    ).then_inc(s_oacy[st_ % 2].h, 16)
                    YX.append(y_oac)
                    oac_region[st_] = (s_oacy[st_ % 2], s_oacy[st_ % 2].reserve(16))
                    stg_last[st_ % 2] = oac_region[st_]
                else:
                    def g_oac(g, dq=dq, row0=row0, oev_now=oev_now, stgbuf=stgbuf, st_=st_,
                              reg_prev=reg_prev, gw_prev=gw_prev):
                        g.wait_ge(s_oev.h, oev_now)
                        if gw_prev:
                            g.wait_ge(s_oac[st_ % 2].h, gw_prev)
                        if reg_prev is not None:
                            g.wait_ge(reg_prev[0].h, reg_prev[1])
                        g.dma_start(out=og_h[row0:row0 + 128, :], in_=stgbuf[:, :],
                                    accum_op=OP.add).then_inc(s_oac[st_ % 2].h, 16)
                    GX.append(g_oac)
                    oac_region[st_] = (s_oac[st_ % 2], s_oac[st_ % 2].reserve(16))
                    stg_last[st_ % 2] = oac_region[st_]
            if pi + 1 < NEXP * NDQ:
                ne, ndq = divmod(pi + 1, NDQ)
                mo_cur = s_mmo.val
                wv_prev2 = s_w.val

                def g_pref2(g, ne=ne, ndq=ndq, mo_cur=mo_cur, wv_prev2=wv_prev2):
                    g.wait_ge(s_w.h, wv_prev2)
                    g.wait_ge(s_mmo.h, mo_cur)
                    g.dma_start(out=w2b[:, :], in_=w2_h[ne, ndq]).then_inc(s_w.h, 16)
                    if ndq == 0:
                        g.dma_start(out=b2b[:, :], in_=b2_h[ne:ne + 1, :]).then_inc(s_w.h, 16)
                GX.append(g_pref2)
                s_w.reserve(16 * (1 + (1 if ndq == 0 else 0)))

        coef_ready = record_p2b(ps4)

        oa0, oa1 = s_oac[0].val, s_oac[1].val
        oy0, oy1 = s_oacy[0].val, s_oacy[1].val

        def g_oacwait(g, oa0=oa0, oa1=oa1, oy0=oy0, oy1=oy1):
            g.wait_ge(s_oac[0].h, oa0)
            g.wait_ge(s_oac[1].h, oa1)
            g.wait_ge(s_oacy[0].h, oy0)
            g.wait_ge(s_oacy[1].h, oy1)
        GX.append(g_oacwait)

        ps4.close()
        barrier()

        ps5 = ExitStack()
        prep = [ps5.enter_context(nc.psum_tensor(f"prep{i}", [128, 512], F32)) for i in range(2)]
        gb5 = [gbufs[:, :], gbuf2[:, :], gbuf3[:, :], gbuf4[:, :]]
        ob5 = [stg[0][:, :], stg[1][:, :], ob3[:, :], ob4[:, :]]
        rep_evac = [0, 0]
        out_free = [0, 0, 0, 0]
        gb_free = [0, 0, 0, 0]
        sct5 = [0, 0, 0, 0]
        xin2 = [0] * (NTILE + NS)
        for _t in range(NS):
            xin2[_t] = y_xload(_t, _t, 0)

        for t in range(NTILE):
            slot = t % NS
            par = t % 2
            xiv = xin2[t]
            gfv = gb_free[slot]

            def g_gath(g, t=t, slot=slot, gfv=gfv):
                g.wait_ge(s_zg.h, 16)
                if gfv:
                    g.wait_ge(s_out.h, gfv)
                g.indirect_dma_start(
                    out=gb5[slot], out_offset=None,
                    in_=og_h, in_offset=IndirectOffsetOnAxis(sloti[:, t:t + 1], 0),
                ).then_inc(s_og[slot].h, 16)
            GX.append(g_gath)
            ogv = s_og[slot].reserve(16)

            rep_need = s_rep.val - 2 * (NS - 1) if t >= NS else 0

            def c_cast(s, t=t, slot=slot, xiv=xiv, rep_need=rep_need):
                s.wait_ge(s_xin[slot].h, xiv)
                if rep_need > 0:
                    s.wait_ge(s_rep.h, rep_need)
                s.copy(nbufs[slot], xbufs[slot]).then_inc(s_xn.h, 1)
            SX.append(c_cast)
            xcv = s_xn.reserve()

            for j in range(2):
                bank = j
                rev = rep_evac[bank]

                def t_rep(e, t=t, j=j, slot=slot, bank=bank, rev=rev, xcv=xcv):
                    e.wait_ge(s_c.h, 160)
                    e.wait_ge(s_xn.h, xcv)
                    if rev:
                        e.wait_ge(s_out.h, rev)
                    e.matmul(prep[bank][:, :], attnF[:, t * 128:(t + 1) * 128],
                             rep64[:, j * 512:(j + 1) * 512],
                             start=True, stop=False)
                    e.matmul(prep[bank][:, :], id128[:, :],
                             nbufs[slot][:, j * 512:(j + 1) * 512],
                             start=False, stop=True).then_inc(s_rep.h, 1)
                TX.append(t_rep)
                rpv = s_rep.reserve()
                ofv = out_free[slot] if t >= NS and j == 0 else 0

                def v_out(v, t=t, j=j, slot=slot, bank=bank, rpv=rpv,
                          ofv=ofv, ogv=ogv):
                    dst = ob5[slot][:, j * 512:(j + 1) * 512]
                    v.wait_ge(s_vg.h, coef_ready)
                    v.wait_ge(s_rep.h, rpv)
                    v.wait_ge(s_og[slot].h, ogv)
                    if ofv:
                        v.wait_ge(s_st[slot].h, ofv)
                    v.scalar_tensor_tensor(dst, gb5[slot][:, j * 512:(j + 1) * 512],
                                           coefb[:, t:t + 1], prep[bank][:, :],
                                           OP.mult, OP.add).then_inc(s_out.h, 1)
                VX.append(v_out)
                rep_evac[bank] = s_out.reserve()
            gb_free[slot] = s_out.val
            ov = s_out.val

            if t % 2 == 0:
                def y_store(y, t=t, slot=slot, ov=ov):
                    y.wait_ge(s_out.h, ov)
                    y.dma_start(out=y_h[t * 128:(t + 1) * 128, :], in_=ob5[slot]
                                ).then_inc(s_st[slot].h, 16)
                YX.append(y_store)
            else:
                def c_store(s, t=t, slot=slot, ov=ov):
                    s.wait_ge(s_out.h, ov)
                    s.dma_start(out=y_h[t * 128:(t + 1) * 128, :], in_=ob5[slot]
                                ).then_inc(s_st[slot].h, 16)
                SX.append(c_store)
            out_free[slot] = s_st[slot].reserve(16)
            if t + NS < NTILE:
                xin2[t + NS] = y_xload(t + NS, (t + NS) % NS, xcv)

        stvals = [s_st[i].val for i in range(4)]

        def y_final(y, stvals=tuple(stvals)):
            for i in range(4):
                y.wait_ge(s_st[i].h, stvals[i])
        YX.append(y_final)
        ps5.close()

        block = stack.enter_context(nc.Block())

        @block.tensor
        def _(e):
            for f in TX:
                f(e)

        @block.vector
        def _(v):
            for f in VX:
                f(v)

        @block.scalar
        def _(s):
            for f in SX:
                f(s)

        @block.gpsimd
        def _(g):
            for f in GX:
                f(g)

        @block.sync
        def _(y):
            for f in YX:
                f(y)

    return nc


_NC_CACHE = None


def kernel(L0_attn_x, L0_attn_Wq, L0_attn_Wk, L0_attn_Wv, L0_ffn_wg,
           L0_ffn_w1, L0_ffn_b1, L0_ffn_w2, L0_ffn_b2):
    global _NC_CACHE
    x = np.ascontiguousarray(np.asarray(L0_attn_x, dtype=np.float32))
    Wq = np.asarray(L0_attn_Wq, np.float32)
    Wk = np.asarray(L0_attn_Wk, np.float32)
    Wv = np.asarray(L0_attn_Wv, np.float32)
    wg = np.asarray(L0_ffn_wg, np.float32)
    w1 = np.asarray(L0_ffn_w1, np.float32)
    b1 = np.asarray(L0_ffn_b1, np.float32)
    w2 = np.asarray(L0_ffn_w2, np.float32)
    b2 = np.ascontiguousarray(np.asarray(L0_ffn_b2, np.float32))

    def t_small(W):
        return np.ascontiguousarray(W.reshape(64, 8, 128).transpose(2, 1, 0))
    wqkvT = np.ascontiguousarray(np.concatenate(
        [t_small(Wq).reshape(128, 512), t_small(Wk).reshape(128, 512),
         t_small(Wv).reshape(128, 512)], axis=1))
    wgT = np.ascontiguousarray(wg.reshape(8, 128, 8).transpose(1, 0, 2).reshape(128, 64))
    b1T = np.ascontiguousarray(b1.reshape(8, 32, 128).transpose(2, 0, 1).reshape(128, 256))
    w1h = np.ascontiguousarray(w1.reshape(8, 8, 128, 4, 1024).transpose(0, 3, 2, 1, 4))
    w2h = np.ascontiguousarray(w2.reshape(8, 4, 8, 128, 1024).transpose(0, 1, 3, 2, 4))

    id128 = np.eye(128, dtype=np.float32)
    lstrict = np.triu(np.ones((128, 128), np.float32), k=1)
    allone = np.ones((128, 128), np.float32)
    rep64 = (np.arange(1024)[None, :] % 64 == np.arange(64)[:, None]).astype(np.float32)

    if _NC_CACHE is None:
        _NC_CACHE = build_program()
    nc = _NC_CACHE

    in_maps = []
    for c in range(N_CORES):
        pre = np.zeros((8, 1), np.float32)
        pre[:c, 0] = 1.0
        hot = np.zeros((8, 1), np.float32)
        hot[c, 0] = 1.0
        in_maps.append({
            "x": x[2 * c:2 * c + 2].reshape(TOK, D),
            "wqkvT": wqkvT, "wgT": wgT, "b1T": b1T, "b2": b2,
            "w1": w1h, "w2": w2h,
            "id128": id128, "lstrict": lstrict, "allone": allone, "rep64": rep64,
            "premask": pre, "myhot": hot,
        })
    res = run_bass_kernel_spmd(nc, in_maps, core_ids=list(range(N_CORES)))
    out = np.concatenate([r["y"].reshape(2, S, D) for r in res.results], axis=0)
    return out
